# version 20
# speedup vs baseline: 1.4299x; 1.0315x over previous
import numpy as np
import concourse.bass as bass
import concourse.mybir as mybir
from concourse.bass_utils import run_bass_kernel_spmd
from contextlib import ExitStack

F32 = mybir.dt.float32
BF16 = mybir.dt.bfloat16
U32 = mybir.dt.uint32
AF = mybir.ActivationFunctionType
ALU = mybir.AluOpType

D = 1024
T = 2048
NCH = 8
NT = 16
NN = 4
H = 16
DK = 64
E = 8
HID = 256
EPS = 1e-5
KB = 1024


class Sched:
    ENGS = ("tensor", "vector", "scalar", "gpsimd", "sync")
    OWN = {"vector": "v", "scalar": "s", "gpsimd": "gc"}

    def __init__(self):
        self.items = {e: [] for e in self.ENGS}
        self.cnt = {}

    def val(self, sem):
        return self.cnt.get(sem, 0)

    def emit(self, eng, fn, waits=(), inc=None, n=1):
        waits = list(waits)
        own = self.OWN.get(eng)
        if own is not None and fn is not None:
            prior = self.val(own)
            if prior > 0:
                waits.append((own, prior))
            if inc is None:
                inc = own
        if inc is not None:
            self.cnt[inc] = self.cnt.get(inc, 0) + n
        self.items[eng].append((tuple(waits), fn, inc, n))
        return self.cnt.get(inc, 0) if inc else None

    def barrier(self, *sems):
        return [(s, self.val(s)) for s in sems if self.val(s) > 0]

    def replay(self, eng_name, eng, sems):
        seen = {}
        for waits, fn, inc, n in self.items[eng_name]:
            for s, v in waits:
                if v > 0 and seen.get(s, -1) < v:
                    eng.wait_ge(sems[s], v)
                    seen[s] = v
            if fn is not None:
                inst = fn(eng)
                if inc is not None:
                    inst.then_inc(sems[inc], n)


def build(stage="full", skips=frozenset()):
    nc = bass.Bass("TRN2", target_bir_lowering=False)

    dt = nc.dram_tensor
    x_tm_h = dt("x_tm", [T, D], F32, kind="ExternalInput").ap()
    x_fm_h = dt("x_fm", [D, T], F32, kind="ExternalInput").ap()
    wq_h = dt("wq", [D, D], F32, kind="ExternalInput").ap()
    wk_h = dt("wk", [D, D], F32, kind="ExternalInput").ap()
    wv_h = dt("wv", [D, D], F32, kind="ExternalInput").ap()
    wfc_h = dt("wfc", [D, D], F32, kind="ExternalInput").ap()
    gam_h = dt("gam", [128, D], F32, kind="ExternalInput").ap()
    bet_h = dt("bet", [128, D], F32, kind="ExternalInput").ap()
    wgT_h = dt("wgT", [D, E], F32, kind="ExternalInput").ap()
    slsT_h = dt("slsT", [D, E], F32, kind="ExternalInput").ap()
    lbT_h = dt("lbT", [D, E], F32, kind="ExternalInput").ap()
    w1_h = dt("w1", [E, D, HID], F32, kind="ExternalInput").ap()
    b1T_h = dt("b1T", [HID, E], F32, kind="ExternalInput").ap()
    w2_h = dt("w2", [E, HID, D], F32, kind="ExternalInput").ap()
    b2_h = dt("b2", [E, D], F32, kind="ExternalInput").ap()
    ident_h = dt("ident", [128, 128], F32, kind="ExternalInput").ap()
    iota_h = dt("iota8", [128, E], F32, kind="ExternalInput").ap()
    esel_h = dt("esel", [E, E * 128], F32, kind="ExternalInput").ap()
    ehead_h = dt("ehead", [H, NCH * 128], F32, kind="ExternalInput").ap()

    out_h = dt("out", [T, D], F32, kind="ExternalOutput").ap()
    dbg = {}
    if stage == "kv":
        dbg["kvtot"] = dt("dbg_kvtot", [65, D], F32, kind="ExternalOutput").ap()
    if stage == "fc":
        dbg["fcb"] = dt("dbg_fcb", [128, NT * D], F32, kind="ExternalOutput").ap()
    if stage in ("x1", "b1solo"):
        dbg["x1"] = dt("dbg_x1", [128, NT * D], F32, kind="ExternalOutput").ap()
        dbg["xhat"] = dt("dbg_xhat", [128, NT * D], F32, kind="ExternalOutput").ap()
        dbg["masks"] = dt("dbg_masks", [E, T], F32, kind="ExternalOutput").ap()

    cc_in = dt("cc_in", [65, D], F32).ap()
    cc_out = dt("cc_out", [65, D], F32).ap()
    fcin_h = dt("fcin", [128, NT * D], F32, kind="ExternalInput").ap()         if stage == "b1solo" else None

    SBASE = 16512
    def sb(name, shape, dtp, off):
        end = off + shape[1] * mybir.dt.size(dtp)
        assert end <= 207 * KB, (name, end)
        return nc.alloc_sbuf_tensor_at(name, shape, dtp, offset=SBASE + off)

    o = [0]
    def nxt(sz):
        r = o[0]; o[0] += sz; return r

    ident_f = sb("ident_f", [128, 128], F32, nxt(512))
    iota8 = sb("iota8s", [128, E], F32, nxt(32))
    esel_b = sb("esel_b", [E, E * 128], BF16, nxt(2 * KB))
    ehead_b = sb("ehead_b", [H, NCH * 128], BF16, nxt(2 * KB))
    ones_b = sb("ones_b", [128, 1], BF16, nxt(32))
    wgT_f = sb("wgT_f", [128, NCH * E], F32, nxt(256))
    slsT_f = sb("slsT_f", [128, NCH * E], F32, nxt(256))
    lbT_b = sb("lbT_b", [128, NCH * E], BF16, nxt(128))
    b1T_f = sb("b1T_f", [128, 2 * E], F32, nxt(64))
    b2b = sb("b2b", [E, D], BF16, nxt(2 * KB))
    bprime = sb("bprime", [128, 2 * E], F32, nxt(64))
    st = sb("stats", [128, 64], F32, nxt(256))
    masksfm = sb("masksfm", [E, T], BF16, nxt(4 * KB))
    off_recip = nxt(4 * KB)
    recip_b = sb("recip_b", [H, T], BF16, off_recip)
    z3a = sb("z3a", [128, D], F32, off_recip)
    dtmp = sb("dtmp", [H, 512], F32, nxt(2 * KB))
    lgt = sb("lgt", [128, E], F32, nxt(32))
    mx8 = sb("mx8", [128, E], F32, nxt(32))
    mtm = sb("mtm", [128, E], F32, nxt(32))
    mrs = sb("mrs", [128, 512], BF16, nxt(KB))
    mrsB = sb("mrsB", [128, 512], BF16, nxt(KB))
    wghi = sb("wghi", [128, NCH * E], BF16, nxt(128))
    wglo = sb("wglo", [128, NCH * E], BF16, nxt(128))
    assert o[0] <= 20 * KB, o[0]

    A = 20 * KB
    xTb = sb("xTb", [128, NCH * T], BF16, A)
    W2b = sb("W2b", [128, E * 2 * D], BF16, A)
    Bo = A + 32 * KB
    WQb = sb("WQb", [128, NCH * D], BF16, Bo)
    Qn = sb("Qn", [128, 2 * NCH * 512], BF16, Bo + 16 * KB)
    xhatF = sb("xhatF", [128, NT * D], BF16, Bo)
    Co = Bo + 32 * KB
    WKb = sb("WKb", [128, NCH * D], BF16, Co)
    WVb = sb("WVb", [128, NCH * D], BF16, Co + 16 * KB)
    W1b = sb("W1b", [128, E * NCH * HID], BF16, Co)
    Dr = Co + 32 * KB
    fcb = sb("fcb", [128, NT * D], BF16, Dr)
    x1F = sb("x1F", [128, NT * D], BF16, Dr)
    E1 = Dr + 32 * KB
    Kt = sb("Kt", [128, 2 * D], BF16, E1)
    Vt = sb("Vt", [128, 2 * D], BF16, E1 + 4 * KB)
    kvacc = sb("kvacc", [65, D], F32, E1 + 8 * KB)
    x_tile = sb("x_tile", [128, 2 * D], F32, E1)
    BDf = sb("BDf", [128, D], F32, E1)
    KSf = sb("KSf", [128, 128], F32, E1 + 4 * KB)
    outb = sb("outb", [128, 2 * D], F32, E1 + 8 * KB)
    E2 = E1 + 16 * KB
    BD_b = sb("BD_b", [128, NCH * 128], BF16, E2)
    KS_b = sb("KS_b", [128, NCH * H], BF16, E2 + 2 * KB)
    z_f32 = sb("z_f32", [128, D], F32, E2)
    z2_f32 = sb("z2_f32", [128, D], F32, E2 + 4 * KB)
    Fo = E2 + 8 * KB
    WFCb = sb("WFCb", [128, NCH * D], BF16, Fo)
    gam_r = sb("gam_r", [128, D], F32, Fo)
    bet_r = sb("bet_r", [128, D], F32, Fo + 4 * KB)
    xh_f32 = sb("xh_f32", [128, D], F32, Fo + 8 * KB)
    xhi_b = sb("xhi_b", [128, D], BF16, Fo + 12 * KB)
    xlo_b = sb("xlo_b", [128, D], BF16, Fo + 14 * KB)
    G2 = Fo + 16 * KB
    attn_n = sb("attn_n", [128, 2 * NCH * 512], BF16, G2)
    hall = sb("hall", [128, E * 2 * 512], BF16, G2)
    zB_f = sb("zB_f32", [128, D], F32, G2)
    z2B_f = sb("z2B_f32", [128, D], F32, G2 + 4 * KB)
    xhB_f = sb("xhB_f32", [128, D], F32, G2 + 8 * KB)
    assert G2 + 16 * KB <= 207 * KB

    S = Sched()
    TT, VV, SS, CC = "t", "v", "s", "c"

    def gdma(sem, dst, src, waits=()):
        return S.emit("gpsimd", lambda g, d=dst, s_=src: g.dma_start(out=d, in_=s_),
                      waits=waits, inc=sem, n=16)

    def ydma(sem, dst, src, waits=()):
        return S.emit("sync", lambda sy, d=dst, s_=src: sy.dma_start(out=d, in_=s_),
                      waits=waits, inc=sem, n=16)

    def fm3(h2d, p=128):
        return h2d.rearrange("(c p) n -> p c n", p=p)

    def dst3(hnd, c):
        return hnd[:, :].rearrange("p (c n) -> p c n", c=c)

    VP = type(x_tm_h.ap)

    def mkap(base, ap_list, offset):
        bb = base.copy()
        bb.ap = VP(ap_list)
        bb.offset = offset
        return bb

    if stage != "b1solo":
        gdma("g_wk", dst3(WKb, NCH), fm3(wk_h))
        gdma("g_xa",
             mkap(xTb[:, :], [[NCH * T, 128], [T, NCH], [1, 1024]], 0),
             mkap(x_fm_h, [[T, 128], [128 * T, NCH], [1, 1024]], 0))
        gdma("g_wv", dst3(WVb, NCH), fm3(wv_h))
        gdma("g_xb",
             mkap(xTb[:, :], [[NCH * T, 128], [T, NCH], [1, 1024]], 1024),
             mkap(x_fm_h, [[T, 128], [128 * T, NCH], [1, 1024]], 1024))
        gdma("g_wq", dst3(WQb, NCH), fm3(wq_h))
        ld_wq = S.val("g_wq")
        gdma("g_wfc", dst3(WFCb, NCH), fm3(wfc_h))
        ld_wfc = S.val("g_wfc")
        gdma("g_eh", ehead_b[:, :], ehead_h[:, :])
        ld_ehead = S.val("g_eh")
    gdma("g_ct", esel_b[:, :], esel_h[:, :])
    gdma("g_ct", dst3(lbT_b, NCH), fm3(lbT_h))
    gdma("g_ct", b2b[:, :], b2_h[:, :])
    ld_consts = S.val("g_ct")

    ydma("y_s1", ident_f[:, :], ident_h[:, :])
    ydma("y_s1", iota8[:, :], iota_h[:, :])
    ydma("y_s1", dst3(wgT_f, NCH), fm3(wgT_h))
    ydma("y_s1", dst3(slsT_f, NCH), fm3(slsT_h))
    ydma("y_s1", dst3(b1T_f, 2), fm3(b1T_h))
    ld_sync1 = S.val("y_s1")

    S.emit("vector", lambda v: v.memset(ones_b[:, :], 1.0), inc=VV)
    ones_done = S.val(VV)
    S.emit("vector", lambda v: v.tensor_copy(wghi[:, :], wgT_f[:, :]),
           waits=[("y_s1", 80)], inc=VV)
    def vWGLO(v):
        with v.bass.allow_low_precision("bf16x2 split of Wg"):
            return v.scalar_tensor_tensor(wglo[:, :], wghi[:, :], -1.0,
                                          wgT_f[:, :], ALU.mult, ALU.add)
    S.emit("vector", vWGLO, inc=VV)
    wg_prep = S.val(VV)
    S.emit("vector", lambda v: v.memset(kvacc[:, :], 0.0), inc=VV)
    kvz = S.val(VV)

    if stage == "b1solo":
        gdma("g_fc", fcb[:, :], fcin_h[:, :])
        ydma("y_gb", gam_r[:, :], gam_h[:, :])
        ydma("y_gb", bet_r[:, :], bet_h[:, :])
        gb_y = S.val("y_gb")
        fcb_evac = [0] * NT
        pipea_t = 0
        num3_t = 0
        w1s_done = 0
        kv_evac = 0
        bd_cast = 0
    else:
        ps1 = ExitStack()
        kp = [ps1.enter_context(nc.psum_tensor(f"kp{i}", [128, D], F32))
              for i in range(2)]
        vp2s = ps1.enter_context(nc.psum_tensor("vp2", [128, D], F32))
        vp2 = [vp2s, vp2s]
        kvps = ps1.enter_context(nc.psum_tensor("kvp", [65, D], F32))
        kvp = [kvps, kvps]

        k_evac = [0] * NT
        v_evac = [0] * NT
        kv_add = [0] * NT
        for tt in range(NT):
            par = tt % 2
            w0 = [("g_wk", 16), ("g_xa", 16)] if tt == 0 else []
            if tt >= 2:
                w0 = [(SS, k_evac[tt - 2])]
            if tt == NT // 2:
                w0 = list(w0) + [("g_xb", 16)]
            for half in range(2):
                for c in range(NCH):
                    def mmK(t, c=c, half=half, tt=tt, par=par):
                        return t.matmul(kp[par][:, half * 512:(half + 1) * 512],
                                        xTb[:, c * T + tt * 128: c * T + tt * 128 + 128],
                                        WKb[:, c * D + half * 512: c * D + half * 512 + 512],
                                        start=(c == 0), stop=(c == NCH - 1))
                    S.emit("tensor", mmK, waits=w0 if (half == 0 and c == 0) else (),
                           inc=TT if (half == 1 and c == NCH - 1) else None)
            k_mm = S.val(TT)
            wv0 = [(VV, v_evac[tt - 1])] if tt >= 1 else [("g_wv", 16)]
            for half in range(2):
                for c in range(NCH):
                    def mmV(t, c=c, half=half, tt=tt, par=par):
                        return t.matmul(vp2[par][:, half * 512:(half + 1) * 512],
                                        xTb[:, c * T + tt * 128: c * T + tt * 128 + 128],
                                        WVb[:, c * D + half * 512: c * D + half * 512 + 512],
                                        start=(c == 0), stop=(c == NCH - 1))
                    S.emit("tensor", mmV, waits=wv0 if (half == 0 and c == 0) else (),
                           inc=TT if (half == 1 and c == NCH - 1) else None)
            v_mm = S.val(TT)
            S.emit("scalar",
                   lambda s_, par=par: s_.activation(Kt[:, par * D:(par + 1) * D],
                                                     kp[par][:, :], AF.Sigmoid),
                   waits=[(TT, k_mm)], inc=SS)
            k_evac[tt] = S.val(SS)
            S.emit("vector",
                   lambda v, par=par: v.tensor_copy(Vt[:, par * D:(par + 1) * D],
                                                    vp2[par][:, :]),
                   waits=[(TT, v_mm)], inc=VV)
            v_evac[tt] = S.val(VV)
            kvw = [(SS, k_evac[tt]), (VV, max(v_evac[tt], ones_done, kvz))]
            if tt >= 1:
                kvw += [(VV, kv_add[tt - 1])]
            for h in range(H):
                def mmKV(t, h=h, par=par, tt=tt):
                    return t.matmul(kvp[par][0:64, h * DK:(h + 1) * DK],
                                    Kt[:, par * D + h * DK: par * D + (h + 1) * DK],
                                    Vt[:, par * D + h * DK: par * D + (h + 1) * DK],
                                    start=True, stop=True)
                S.emit("tensor", mmKV, waits=kvw if h == 0 else ())
            for half in range(2):
                def mmKS(t, half=half, par=par, tt=tt):
                    return t.matmul(kvp[par][64:65, half * 512:(half + 1) * 512],
                                    ones_b[:, :],
                                    Kt[:, par * D + half * 512: par * D + half * 512 + 512],
                                    start=True, stop=True)
                S.emit("tensor", mmKS, inc=TT if half == 1 else None)
            kv_mm = S.val(TT)
            S.emit("vector",
                   lambda v, par=par: v.tensor_tensor(kvacc[:, :], kvacc[:, :],
                                                      kvp[par][:, :], ALU.add),
                   waits=[(TT, kv_mm)], inc=VV)
            kv_add[tt] = S.val(VV)
        p1_t = S.val(TT)
        kv_evac = kv_add[NT - 1]
        ps1.close()

        gdma("g_pk", cc_in[:, :], kvacc[:, :], waits=[(VV, kv_evac)])
        pack_g = S.val("g_pk")

        def coll(g):
            return g.collective_compute(
                "AllReduce", ALU.add,
                replica_groups=[[0, 1], [2, 3], [4, 5], [6, 7]],
                ins=[cc_in[:, :]], outs=[cc_out[:, :]])
        S.emit("gpsimd", coll, waits=[("g_pk", pack_g)], inc=CC)

        S.emit("vector", lambda v: v.memset(BDf[:, :], 0.0),
               waits=[(TT, p1_t), (SS, k_evac[NT - 1])], inc=VV)
        S.emit("vector", lambda v: v.memset(KSf[:, :], 0.0), inc=VV)
        bdks_zero = S.val(VV)

        for h2 in range(2):
            ydma("y_sc",
                 mkap(BDf[:, :], [[1024, 64], [128, 8], [1, 64]], h2 * 64 * 1024 + h2 * 64),
                 mkap(cc_out[:, :], [[1024, 64], [128, 8], [1, 64]], h2 * 64),
                 waits=[(CC, 1), (VV, bdks_zero)] if h2 == 0 else ())
            S.emit("sync",
                   lambda sy, d=mkap(KSf[:, :], [[128, 64], [17, 8]],
                                     h2 * 64 * 128 + 8 * h2),
                   s_=mkap(cc_out[:, :], [[1, 64], [128, 8]], 64 * 1024 + h2 * 64):
                   sy.dma_start(out=d, in_=s_, allow_slow_non_contiguous=True),
                   inc="y_sc", n=16)
        sc_y = S.val("y_sc")
        S.emit("vector", lambda v: v.tensor_copy(KS_b[:, :], KSf[:, :]),
               waits=[("y_sc", sc_y)], inc=VV)
        ks_cast = S.val(VV)
        S.emit("vector", lambda v: v.tensor_copy(BD_b[:, :], BDf[:, :]), inc=VV)
        bd_cast = S.val(VV)

        if stage == "kv":
            ydma("y_dbg", dbg["kvtot"][0:64, :], cc_out[0:64, :], waits=[(VV, bd_cast)])
            ydma("y_dbg", dbg["kvtot"][64:65, :], cc_out[64:65, :])
            S.emit("sync", lambda sy: sy.nop(), waits=[("y_dbg", S.val("y_dbg"))])
            return finish(nc, S)

        psA = ExitStack()
        qp = [psA.enter_context(nc.psum_tensor(f"qp{i}", [128, 512], F32)) for i in range(2)]
        rbp = [psA.enter_context(nc.psum_tensor(f"rbp{i}", [128, 512], F32)) for i in range(2)]
        nump = [psA.enter_context(nc.psum_tensor(f"nump{i}", [128, 512], F32)) for i in range(2)]
        dnp = rbp[0][0:H, :]
        fcp = psA.enter_context(nc.psum_tensor("fcp", [128, D], F32))

        q_evac = {}
        fcb_evac = [0] * NT

        def emit_Q(n):
            npar = n % 2
            for m in range(NCH):
                idx = n * NCH + m
                w = []
                if idx == 0:
                    w = [(SS, k_evac[NT - 1]), (VV, kv_evac),
                         ("g_wq", ld_wq)]
                if idx >= 2:
                    w += [(SS, q_evac[idx - 2])]
                for c in range(NCH):
                    def mmQ(t, m=m, c=c, n=n):
                        return t.matmul(qp[m % 2][:, :],
                                        WQb[:, c * D + m * 128: c * D + m * 128 + 128],
                                        xTb[:, c * T + n * 512: c * T + n * 512 + 512],
                                        start=(c == 0), stop=(c == NCH - 1))
                    S.emit("tensor", mmQ, waits=w if c == 0 else (),
                           inc=TT if c == NCH - 1 else None)
                qmm = S.val(TT)
                S.emit("scalar",
                       lambda s_, m=m, npar=npar: s_.activation(
                           Qn[:, npar * 4096 + m * 512: npar * 4096 + (m + 1) * 512],
                           qp[m % 2][:, :], AF.Sigmoid),
                       waits=[(TT, qmm)], inc=SS)
                q_evac[idx] = S.val(SS)

        emit_Q(0)
        emit_Q(1)
        attn_prev = 0
        rb_prev = 0
        mrsP = [mrs, mrsB]
        recipd = {}

        def emit_den(n, extra=()):
            npar = n % 2
            dw = [(SS, q_evac[n * NCH + NCH - 1])] + list(extra)
            for c in range(NCH):
                def mmD(t, c=c, npar=npar):
                    return t.matmul(dnp[:, :],
                                    KS_b[:, c * H:(c + 1) * H],
                                    Qn[:, npar * 4096 + c * 512: npar * 4096 + (c + 1) * 512],
                                    start=(c == 0), stop=(c == NCH - 1))
                S.emit("tensor", mmD, waits=dw if c == 0 else (),
                       inc=TT if c == NCH - 1 else None)
            dmm = S.val(TT)
            S.emit("vector", lambda v: v.tensor_scalar(dtmp[:, :], dnp[:, :], 1.0, 1e-8,
                                                       ALU.mult, ALU.add),
                   waits=[(TT, dmm)], inc=VV)
            def vRC(v, n=n):
                with v.bass.allow_low_precision("bf16 matmul operand for rb bcast"):
                    return v.reciprocal(recip_b[:, n * 512:(n + 1) * 512], dtmp[:, :])
            S.emit("vector", vRC, inc=VV)
            recipd[n] = S.val(VV)

        emit_den(0, [(VV, ks_cast)])
        for n in range(NN):
            npar = n % 2
            recip_done = recipd[n]
            rb_evac = {}
            at_evac = {}
            for c in range(NCH):
                w = []
                if c <= 1:
                    w += [(VV, recip_done)]
                    if rb_prev:
                        w += [(SS, rb_prev)]
                    if n == 0 and c == 0:
                        w += [("g_eh", ld_ehead)]
                else:
                    w += [(SS, rb_evac[c - 2])]
                def mmRB(t, c=c, n=n):
                    return t.matmul(rbp[c % 2][:, :], ehead_b[:, c * 128:(c + 1) * 128],
                                    recip_b[:, n * 512:(n + 1) * 512],
                                    start=True, stop=True)
                S.emit("tensor", mmRB, waits=w, inc=TT)
                rbmm = S.val(TT)
                S.emit("scalar", lambda s_, c=c: s_.copy(mrsP[c % 2][:, :], rbp[c % 2][:, :]),
                       waits=[(TT, rbmm), (VV, at_evac.get(c - 2, attn_prev))], inc=SS)
                rb_evac[c] = S.val(SS)
                nw = []
                if c <= 1:
                    nw += [(VV, recip_done)]
                    if n == 0:
                        nw += [(VV, bd_cast)]
                else:
                    nw += [(VV, at_evac[c - 2])]
                def mmNUM(t, c=c, npar=npar):
                    return t.matmul(nump[c % 2][:, :], BD_b[:, c * 128:(c + 1) * 128],
                                    Qn[:, npar * 4096 + c * 512: npar * 4096 + (c + 1) * 512],
                                    start=True, stop=True)
                S.emit("tensor", mmNUM, waits=nw, inc=TT)
                nmm = S.val(TT)
                def vAT(v, c=c, npar=npar):
                    return v.scalar_tensor_tensor(
                        attn_n[:, npar * 4096 + c * 512: npar * 4096 + (c + 1) * 512],
                        nump[c % 2][:, :], 1.0, mrsP[c % 2][:, :], ALU.mult, ALU.mult)
                S.emit("vector", vAT, waits=[(TT, nmm), (SS, rb_evac[c])], inc=VV)
                at_evac[c] = S.val(VV)
            attn_done = at_evac[NCH - 1]
            attn_prev = attn_done
            rb_prev = rb_evac[NCH - 1]
            if n == NN - 1:
                num3_t = S.val(TT)
            if n + 1 < NN:
                emit_den(n + 1, [(SS, rb_evac[NCH - 1])])
            for i in range(4):
                tt = n * 4 + i
                w = [(VV, attn_done)]
                if n == 0 and i == 0:
                    w += [("g_wfc", ld_wfc)]
                if tt >= 1:
                    w += [(SS, fcb_evac[tt - 1])]
                for half in range(2):
                    for c in range(NCH):
                        def mmFC(t, c=c, half=half, i=i, npar=npar):
                            return t.matmul(
                                fcp[:, half * 512:(half + 1) * 512],
                                attn_n[:, npar * 4096 + c * 512 + i * 128:
                                       npar * 4096 + c * 512 + i * 128 + 128],
                                WFCb[:, c * D + half * 512: c * D + half * 512 + 512],
                                start=(c == 0), stop=(c == NCH - 1))
                        S.emit("tensor", mmFC,
                               waits=w if (half == 0 and c == 0) else (),
                               inc=TT if (half == 1 and c == NCH - 1) else None)
                fmm = S.val(TT)
                S.emit("scalar",
                       lambda s_, tt=tt: s_.copy(fcb[:, tt * D:(tt + 1) * D], fcp[:, :]),
                       waits=[(TT, fmm)], inc=SS)
                fcb_evac[tt] = S.val(SS)
            if n + 2 < NN:
                emit_Q(n + 2)
        pipea_t = S.val(TT)
        psA.close()

        gdma("g_w1", W1b[:, :].rearrange("p (e c h) -> p e c h", e=E, c=NCH),
             w1_h.rearrange("e (c p) h -> p e c h", p=128),
             waits=[(TT, p1_t)])
        w1_g = S.val("g_w1")

        ydma("y_gb", gam_r[:, :], gam_h[:, :], waits=[(TT, pipea_t)])
        ydma("y_gb", bet_r[:, :], bet_h[:, :])
        gb_y = S.val("y_gb")

        if stage == "fc":
            gdma("g_dbg", dbg["fcb"][:, :], fcb[:, :], waits=[(SS, fcb_evac[NT - 1])])
            S.emit("gpsimd", lambda g: g.nop(), waits=[("g_dbg", S.val("g_dbg"))])
            return finish(nc, S)


    gdma("g_w2", W2b[:, :].rearrange("p (e ch n) -> p e ch n", e=E, ch=2),
         w2_h.rearrange("e (ch p2) n -> p2 e ch n", p2=128),
         waits=[(TT, pipea_t)])
    w2_g = [S.val("g_w2")]

    psB = ExitStack()
    xppA = psB.enter_context(nc.psum_tensor("xppA", [128, D], F32))
    xppB = psB.enter_context(nc.psum_tensor("xppB", [128, D], F32))
    pb = psB.enter_context(nc.psum_tensor("pb", [128, 256], F32))
    bpp = psB.enter_context(nc.psum_tensor("bpp", [128, 2], F32))
    lgp = pb[:, 0:E]
    mkp = pb[0:E, 128:256]

    b1_bar = S.barrier(SS, VV) + ([(TT, num3_t)] if num3_t else [])
    if stage == "b1solo":
        b1_bar = b1_bar + [("g_fc", 16)]

    COLS = {"zs": 0, "nm1": 1, "t1": 2, "s1": 3, "r1": 4, "m1r1": 5,
            "z2s": 6, "nm2": 7, "a": 8, "sa": 9, "r2": 10, "ra": 11,
            "v2": 12, "v3": 13, "c3": 14, "s3": 15, "r3": 16, "r23": 17,
            "n2r2": 18, "n2r23": 19, "z3s": 20, "nm4": 21, "a4": 22,
            "s4": 23, "r4": 24, "n4r4": 25}

    def stc(name, tt):
        base = (tt % 2) * 32 + COLS[name]
        return st[:, base:base + 1]

    x_loaded = [0] * (NT + 2)
    x_free = [0] * NT
    mask_done = [0] * NT
    sq1_done = [0] * NT
    sq2_done = [0] * NT
    tp_done = [0] * NT
    xh_done = [0] * NT
    xh_val = [0] * NT
    hilo_d = [0] * NT
    mkd_d = [0] * NT
    for tt in range(2):
        xw = [(VV, kv_evac if stage == "b1solo" else max(kv_evac, bd_cast))]
        ydma(f"y_x{tt % 2}", x_tile[:, (tt % 2) * D:(tt % 2 + 1) * D],
             x_tm_h[tt * 128:(tt + 1) * 128, :], waits=xw)
        x_loaded[tt] = S.val(f"y_x{tt % 2}")

    def stretch_gen(tt):
        par = tt % 2
        zb_ = z_f32 if par == 0 else zB_f
        z2_ = z2_f32 if par == 0 else z2B_f
        xh_ = xh_f32 if par == 0 else xhB_f
        xpp = xppA if par == 0 else xppB
        w = [(f"y_x{par}", x_loaded[tt]), (SS, fcb_evac[tt])]
        if tt == 0:
            w += list(b1_bar)
        if tt >= 2:
            w += [(VV, x_free[tt - 2])]

        def vZ(v, tt=tt, par=par, zb_=zb_):
            return v.scalar_tensor_tensor(
                zb_[:, :], fcb[:, tt * D:(tt + 1) * D], 1.0,
                x_tile[:, par * D:(par + 1) * D],
                ALU.mult, ALU.add, accum_out=stc("zs", tt))
        S.emit("vector", vZ, waits=w, inc=VV)
        v1c = S.val(VV)
        S.emit("scalar", lambda s_, tt=tt: s_.mul(stc("nm1", tt), stc("zs", tt),
                                                  -1.0 / D),
               waits=[(VV, v1c)], inc=SS)
        def sSq(s_, tt=tt, zb_=zb_):
            return s_.activation(fcb[:, tt * D:(tt + 1) * D], zb_[:, :],
                                 AF.Square, bias=stc("nm1", tt),
                                 accum_out=stc("t1", tt))
        S.emit("scalar", sSq, inc=SS)
        sq1_done[tt] = S.val(SS)
        S.emit("scalar", lambda s_, tt=tt: s_.activation(
            stc("t1", tt), stc("t1", tt), AF.Copy, scale=1.0 / D, bias=EPS),
            inc=SS)
        S.emit("scalar", lambda s_, tt=tt: s_.activation(
            stc("s1", tt), stc("t1", tt), AF.Sqrt), inc=SS)
        s1d = S.val(SS)
        yield

        S.emit("vector", lambda v, tt=tt: v.reciprocal(stc("r1", tt), stc("s1", tt)),
               waits=[(SS, s1d)], inc=VV)
        S.emit("vector", lambda v, tt=tt: v.tensor_tensor(
            stc("m1r1", tt), stc("nm1", tt), stc("r1", tt), ALU.mult), inc=VV)
        S.emit("vector", lambda v, tt=tt, zb_=zb_: v.tensor_scalar(
            zb_[:, :], zb_[:, :], stc("r1", tt), stc("m1r1", tt),
            ALU.mult, ALU.add), inc=VV)
        S.emit("vector", lambda v, zb_=zb_: v.tensor_tensor(
            zb_[:, :], zb_[:, :], gam_r[:, :], ALU.mult),
            waits=[("y_gb", gb_y)], inc=VV)
        S.emit("vector", lambda v, zb_=zb_: v.tensor_tensor(
            zb_[:, :], zb_[:, :], bet_r[:, :], ALU.add), inc=VV)
        yield

        w2w = []
        if tt >= 2:
            w2w += [(SS, sq2_done[tt - 2]), (VV, xh_val[tt - 2])]
        def vZ2(v, tt=tt, par=par, zb_=zb_, z2_=z2_):
            return v.scalar_tensor_tensor(
                z2_[:, :], zb_[:, :], 1.0,
                x_tile[:, par * D:(par + 1) * D],
                ALU.mult, ALU.add, accum_out=stc("z2s", tt))
        S.emit("vector", vZ2, waits=w2w, inc=VV)
        x_free[tt] = S.val(VV)
        S.emit("scalar", lambda s_, tt=tt: s_.mul(stc("nm2", tt), stc("z2s", tt),
                                                  -1.0 / D),
               waits=[(VV, x_free[tt])], inc=SS)
        if tt + 2 < NT:
            ydma(f"y_x{par}", x_tile[:, par * D:(par + 1) * D],
                 x_tm_h[(tt + 2) * 128:(tt + 3) * 128, :],
                 waits=[(VV, x_free[tt])])
            x_loaded[tt + 2] = S.val(f"y_x{par}")
        def sSq2(s_, tt=tt, z2_=z2_):
            return s_.activation(fcb[:, tt * D:(tt + 1) * D], z2_[:, :],
                                 AF.Square, bias=stc("nm2", tt),
                                 accum_out=stc("a", tt))
        S.emit("scalar", sSq2, inc=SS)
        sq2_done[tt] = S.val(SS)
        yield

        S.emit("scalar", lambda s_, tt=tt: s_.mul(stc("v2", tt), stc("a", tt),
                                                  1.0 / D), inc=SS)
        S.emit("scalar", lambda s_, tt=tt: s_.activation(
            stc("a", tt), stc("v2", tt), AF.Copy, bias=EPS), inc=SS)
        S.emit("scalar", lambda s_, tt=tt: s_.activation(
            stc("sa", tt), stc("a", tt), AF.Sqrt), inc=SS)
        sad = S.val(SS)
        yield

        S.emit("vector", lambda v, tt=tt: v.reciprocal(stc("r2", tt), stc("sa", tt)),
               waits=[(SS, sad)], inc=VV)
        S.emit("vector", lambda v, tt=tt: v.reciprocal(stc("ra", tt), stc("a", tt)),
               inc=VV)
        rac = S.val(VV)
        yield

        S.emit("scalar", lambda s_, tt=tt: s_.activation(
            stc("v3", tt), stc("v2", tt), AF.Copy, scale=stc("ra", tt)),
            waits=[(VV, rac)], inc=SS)
        S.emit("scalar", lambda s_, tt=tt: s_.activation(
            stc("c3", tt), stc("v3", tt), AF.Copy, bias=EPS), inc=SS)
        S.emit("scalar", lambda s_, tt=tt: s_.activation(
            stc("s3", tt), stc("c3", tt), AF.Sqrt), inc=SS)
        s3d = S.val(SS)
        yield

        S.emit("vector", lambda v, tt=tt: v.reciprocal(stc("r3", tt), stc("s3", tt)),
               waits=[(SS, s3d)], inc=VV)
        S.emit("vector", lambda v, tt=tt: v.tensor_tensor(
            stc("r23", tt), stc("r2", tt), stc("r3", tt), ALU.mult), inc=VV)
        S.emit("vector", lambda v, tt=tt: v.tensor_tensor(
            stc("n2r2", tt), stc("nm2", tt), stc("r2", tt), ALU.mult), inc=VV)
        S.emit("vector", lambda v, tt=tt: v.tensor_tensor(
            stc("n2r23", tt), stc("nm2", tt), stc("r23", tt), ALU.mult), inc=VV)
        yield

        def vX1(v, tt=tt, z2_=z2_):
            return v.tensor_scalar(
                x1F[:, tt * D:(tt + 1) * D],
                z2_[:, :], stc("r2", tt), stc("n2r2", tt), ALU.mult, ALU.add)
        S.emit("vector", vX1, inc=VV)

        def vXH(v, tt=tt, z2_=z2_, xh_=xh_):
            return v.tensor_scalar(
                xh_[:, :], z2_[:, :], stc("r23", tt), stc("n2r23", tt),
                ALU.mult, ALU.add)
        S.emit("vector", vXH,
               waits=[(TT, tp_done[tt - 2])] if tt >= 2 else (), inc=VV)
        xh_val[tt] = S.val(VV)
        yield

        tw = [(VV, xh_val[tt]), ("y_s1", ld_sync1)]
        if tt >= 2:
            tw += [(SS, xh_done[tt - 2]), (VV, hilo_d[tt - 2])]
        for c in range(NCH):
            def mmT(t, c=c, xh_=xh_):
                return t.transpose(xpp[:, c * 128:(c + 1) * 128],
                                   xh_[:, c * 128:(c + 1) * 128], ident_f[:, :])
            S.emit("tensor", mmT, waits=tw if c == 0 else (),
                   inc=TT if c == NCH - 1 else None)
        tp_done[tt] = S.val(TT)
        yield

        def vXE(s_, tt=tt):
            q, i = tt // 4, tt % 4
            out_ap = mkap(xhatF[:, :], [[16384, 128], [512, 8], [1, 128]],
                          q * 4096 + i * 128)
            in_ap = xpp[:, :].rearrange("p (c n) -> p c n", c=NCH)
            return s_.copy(out_ap, in_ap)
        S.emit("scalar", vXE, waits=[(TT, tp_done[tt])], inc=SS)
        xh_done[tt] = S.val(SS)
        if "logits" in skips:
            mask_done[tt] = S.val(VV)
            hilo_d[tt] = S.val(VV)
            mkd_d[tt] = tp_done[tt]
            return
        def vLO(v, tt=tt):
            q, i = tt // 4, tt % 4
            hi_ap = mkap(xhatF[:, :], [[16384, 128], [512, 8], [1, 128]],
                         q * 4096 + i * 128)
            with v.bass.allow_low_precision("bf16x2 split"):
                return v.scalar_tensor_tensor(xlo_b[:, :], hi_ap, -1.0,
                                              xpp[:, :], ALU.mult, ALU.add)
        S.emit("vector", vLO, waits=[(SS, xh_done[tt])], inc=VV)
        hilo_d[tt] = S.val(VV)
        yield

        lw = [(VV, max(hilo_d[tt], wg_prep))]
        if tt >= 1:
            lw += [(VV, mask_done[tt - 1])]
        qq, ii = tt // 4, tt % 4
        lg_ops = []
        for c in range(NCH):
            hi_sl = (xhatF, qq * 4096 + c * 512 + ii * 128)
            lg_ops.append((hi_sl, wghi, c))
            lg_ops.append((hi_sl, wglo, c))
            lg_ops.append(((xlo_b, c * 128), wghi, c))
        for i2, ((xb_, xoff), wb_, c) in enumerate(lg_ops):
            def mmLG(t, xb_=xb_, xoff=xoff, wb_=wb_, c=c, i2=i2):
                return t.matmul(lgp, xb_[:, xoff:xoff + 128],
                                wb_[:, c * E:(c + 1) * E],
                                start=(i2 == 0), stop=(i2 == len(lg_ops) - 1))
            S.emit("tensor", mmLG, waits=lw if i2 == 0 else (),
                   inc=TT if i2 == len(lg_ops) - 1 else None)
        lgd = S.val(TT)
        S.emit("vector", lambda v: v.tensor_copy(lgt[:, :], lgp),
               waits=[(TT, lgd)], inc=VV)
        S.emit("vector", lambda v: v.reduce_max(
            mx8[:, 0:1], lgt[:, :], axis=mybir.AxisListType.X), inc=VV)
        mxd = S.val(VV)
        eqw = [(VV, mxd)]
        if tt >= 1:
            eqw += [(TT, mkd_d[tt - 1])]
        def vEQ(v):
            with v.bass.allow_low_precision("0/1 mask is exact in bf16"):
                return v.tensor_scalar(
                    mtm[:, :], lgt[:, :], mx8[:, 0:1], None, ALU.is_equal)
        S.emit("vector", vEQ, waits=eqw, inc=VV)
        mtmd = S.val(VV)
        mtw = [(VV, mtmd)]
        if tt >= 1:
            mtw += [(VV, mask_done[tt - 1])]
        S.emit("tensor", lambda t: t.transpose(mkp, mtm[:, :], ident_f[:, :]),
               waits=mtw, inc=TT)
        mkd_d[tt] = S.val(TT)
        S.emit("vector", lambda v, tt=tt: v.tensor_copy(
            masksfm[:, tt * 128:(tt + 1) * 128], mkp),
            waits=[(TT, mkd_d[tt])], inc=VV)
        mask_done[tt] = S.val(VV)

    w1s_done = 0
    bp_prev = [0, 0]
    def emit_w1prep(e):
        for ch in range(2):
            w = []
            if e == 0 and ch == 0:
                w = [("g_w1", w1_g), ("g_ct", ld_consts),
                     (SS, fcb_evac[NT - 1])]
            if bp_prev[ch]:
                w = list(w) + [(VV, bp_prev[ch])]
            for c in range(NCH):
                def mmBP(t, e=e, ch=ch, c=c):
                    col = (e * NCH + c) * HID + ch * 128
                    return t.matmul(bpp[:, ch:ch + 1],
                                    W1b[:, col:col + 128],
                                    lbT_b[:, c * E + e: c * E + e + 1],
                                    start=(c == 0), stop=(c == NCH - 1))
                S.emit("tensor", mmBP, waits=w if c == 0 else (),
                       inc=TT if c == NCH - 1 else None)
            bmm = S.val(TT)
            S.emit("vector",
                   lambda v, e=e, ch=ch: v.tensor_tensor(
                       bprime[:, e * 2 + ch: e * 2 + ch + 1],
                       bpp[:, ch:ch + 1], b1T_f[:, ch * E + e: ch * E + e + 1],
                       ALU.add),
                   waits=[(TT, bmm), ("y_s1", ld_sync1)], inc=VV)
            bp_prev[ch] = S.val(VV)
        bmm2 = S.val(TT)
        for c in range(NCH):
            def sSC(s_, e=e, c=c):
                col = (e * NCH + c) * HID
                return s_.activation(W1b[:, col:col + HID], W1b[:, col:col + HID],
                                     AF.Copy, scale=slsT_f[:, c * E + e: c * E + e + 1])
            S.emit("scalar", sSC, waits=[(TT, bmm2)] if c == 0 else (),
                   inc=SS if c == NCH - 1 else None)
        return S.val(SS)

    def drive(g):
        try:
            next(g)
        except StopIteration:
            pass

    for k in range(0, NT, 2):
        ga = stretch_gen(k)
        gb = stretch_gen(k + 1)
        drive(ga); drive(gb)
        if stage != "b1solo" and "w1prep" not in skips and k // 2 < E:
            w1s_done = emit_w1prep(k // 2)
        for who in (ga, gb, ga, gb, ga, ga, ga, ga, ga, gb, gb, gb, gb, gb,
                    ga, gb, ga, ga, gb, gb):
            drive(who)
        for g in (ga, gb):
            for _ in range(8):
                drive(g)

    b1_v = S.val(VV)
    b1_s = S.val(SS)
    psB.close()

    if stage in ("x1", "b1solo"):
        gdma("g_dbg", dbg["x1"][:, :], x1F[:, :], waits=[(VV, b1_v), (SS, b1_s)])
        gdma("g_dbg", dbg["xhat"][:, :], xhatF[:, :])
        gdma("g_dbg", dbg["masks"][:, :], masksfm[:, :])
        S.emit("gpsimd", lambda g: g.nop(), waits=[("g_dbg", S.val("g_dbg"))])
        return finish(nc, S)

    psB2 = ExitStack()
    hp = [psB2.enter_context(nc.psum_tensor(f"hp{i}", [128, 512], F32))
          for i in range(4)]
    yp = [psB2.enter_context(nc.psum_tensor(f"yp{i}", [128, D], F32))
          for i in range(2)]

    b2_bar = [(VV, b1_v), (SS, b1_s)]

    z3_used = [0] * NT
    sq4_done = [0] * NT
    nm4_done = [0] * NT
    tail_done = {}
    out_dma = [0] * NT

    def emit_tail(tt):
        zb = z3a
        def sSq4(s_, tt=tt, zb=zb):
            return s_.activation(x1F[:, tt * D:(tt + 1) * D], zb[:, :],
                                 AF.Square, bias=stc("nm4", tt),
                                 accum_out=stc("a4", tt))
        S.emit("scalar", sSq4,
               waits=[(VV, z3_used[tt]), (SS, nm4_done[tt])], inc=SS)
        sq4c = S.val(SS)
        S.emit("scalar", lambda s_, tt=tt: s_.activation(
            stc("a4", tt), stc("a4", tt), AF.Copy, bias=EPS, scale=1.0 / D),
            waits=[(SS, sq4c)], inc=SS)
        S.emit("scalar", lambda s_, tt=tt: s_.activation(
            stc("s4", tt), stc("a4", tt), AF.Sqrt), inc=SS)
        sq4_done[tt] = S.val(SS)
        s4d = S.val(SS)
        S.emit("vector", lambda v, tt=tt: v.reciprocal(stc("r4", tt), stc("s4", tt)),
               waits=[(SS, s4d)], inc=VV)
        r4d = S.val(VV)
        S.emit("scalar", lambda s_, tt=tt: s_.activation(
            stc("n4r4", tt), stc("nm4", tt), AF.Copy, scale=stc("r4", tt)),
            waits=[(VV, r4d)], inc=SS)
        n4d = S.val(SS)
        ow = [(SS, n4d)]
        if tt >= 2:
            ow += [(f"y_o{tt % 2}", out_dma[tt - 2])]
        def vOUT(v, tt=tt, zb=zb, par=tt % 2):
            return v.tensor_scalar(
                outb[:, par * D:(par + 1) * D], zb[:, :],
                stc("r4", tt), stc("n4r4", tt), ALU.mult, ALU.add)
        S.emit("vector", vOUT, waits=ow + [(VV, r4d)], inc=VV)
        od = S.val(VV)
        tail_done[tt] = od
        ydma(f"y_o{tt % 2}", out_h[tt * 128:(tt + 1) * 128, :],
             outb[:, (tt % 2) * D:(tt % 2 + 1) * D], waits=[(VV, od)])
        out_dma[tt] = S.val(f"y_o{tt % 2}")

    for q in range(NN):
        gelu_d = {}
        msk_d = {}
        mrep_d = {}
        for e in range(E):
            epar = e % 2
            w = [(VV, mask_done[q * 4 + 3])]
            if q == 0 and e == 0:
                w += list(b2_bar)
            if e >= 2:
                w += [(SS, mrep_d[e - 2])]
            elif q >= 1:
                w += [(VV, z3_used[q * 4 - 2])]
            def mmMR(t, e=e, q=q, epar=epar):
                return t.matmul(yp[0][:, epar * 512:(epar + 1) * 512],
                                esel_b[:, e * 128:(e + 1) * 128],
                                masksfm[:, q * 512:(q + 1) * 512],
                                start=True, stop=True)
            S.emit("tensor", mmMR, waits=w, inc=TT)
            mrd = S.val(TT)
            S.emit("scalar", lambda s_, epar=epar: s_.copy(
                mrsP[epar][:, :], yp[0][:, epar * 512:(epar + 1) * 512]),
                   waits=[(TT, mrd), (VV, msk_d.get(e - 2, 0))], inc=SS)
            mrep_d[e] = S.val(SS)
            hw = [(SS, max(xh_done[q * 4 + 3], w1s_done))]
            if e >= 2:
                hw += [(SS, gelu_d[e - 2])]
            for ch in range(2):
                for c in range(NCH):
                    def mmH(t, e=e, ch=ch, c=c, q=q, epar=epar):
                        col = (e * NCH + c) * HID + ch * 128
                        return t.matmul(
                            hp[epar * 2 + ch][:, :],
                            W1b[:, col:col + 128],
                            xhatF[:, q * 4096 + c * 512: q * 4096 + (c + 1) * 512],
                            start=(c == 0), stop=(c == NCH - 1))
                    S.emit("tensor", mmH,
                           waits=hw if (ch == 0 and c == 0) else (),
                           inc=TT if (ch == 1 and c == NCH - 1) else None)
            hmm = S.val(TT)
            for ch in range(2):
                def sGE(s_, e=e, ch=ch, epar=epar):
                    return s_.activation(
                        hall[:, (e * 2 + ch) * 512:(e * 2 + ch + 1) * 512],
                        hp[epar * 2 + ch][:, :], AF.Gelu,
                        bias=bprime[:, e * 2 + ch:e * 2 + ch + 1])
                S.emit("scalar", sGE, waits=[(TT, hmm)] if ch == 0 else (),
                       inc=SS if ch == 1 else None)
            gelu_d[e] = S.val(SS)
            for ch in range(2):
                def vMM(v, e=e, ch=ch, epar=epar):
                    sl = hall[:, (e * 2 + ch) * 512:(e * 2 + ch + 1) * 512]
                    return v.tensor_tensor(sl, sl, mrsP[epar][:, :], ALU.mult)
                S.emit("vector", vMM,
                       waits=[(SS, max(gelu_d[e], mrep_d[e]))] if ch == 0 else (),
                       inc=VV if ch == 1 else None)
            msk_d[e] = S.val(VV)

        for i in range(4):
            tt = q * 4 + i
            ypt = yp[tt % 2]
            w = [(VV, msk_d[E - 1]), ("g_w2", max(w2_g)), ("g_ct", ld_consts),
                 (SS, mrep_d[E - 1])]
            if tt >= 2:
                w += [(VV, z3_used[tt - 2])]
            for nh in range(2):
                def mmB2(t, tt=tt, nh=nh, ypt=ypt):
                    return t.matmul(ypt[:, nh * 512:(nh + 1) * 512],
                                    masksfm[:, tt * 128:(tt + 1) * 128],
                                    b2b[:, nh * 512:(nh + 1) * 512],
                                    start=True, stop=False)
                S.emit("tensor", mmB2, waits=w if nh == 0 else ())
            for e in range(E):
                for ch in range(2):
                    for nh in range(2):
                        def mmY(t, e=e, ch=ch, nh=nh, i=i, ypt=ypt):
                            hcol = (e * 2 + ch) * 512 + i * 128
                            wcol = (e * 2 + ch) * D + nh * 512
                            return t.matmul(
                                ypt[:, nh * 512:(nh + 1) * 512],
                                hall[:, hcol:hcol + 128],
                                W2b[:, wcol:wcol + 512],
                                start=False,
                                stop=(e == E - 1 and ch == 1))
                        S.emit("tensor", mmY,
                               inc=TT if (e == E - 1 and ch == 1 and nh == 1) else None)
            ymm = S.val(TT)
            zb = z3a
            vz3w = [(TT, ymm)]
            if tt >= 1:
                vz3w += [(SS, sq4_done[tt - 1])]
            def vZ3(v, tt=tt, zb=zb, ypt=ypt):
                return v.scalar_tensor_tensor(
                    zb[:, :], ypt[:, :], 1.0,
                    x1F[:, tt * D:(tt + 1) * D],
                    ALU.mult, ALU.add, accum_out=stc("z3s", tt))
            S.emit("vector", vZ3, waits=vz3w, inc=VV)
            z3_used[tt] = S.val(VV)
            S.emit("scalar", lambda s_, tt=tt: s_.mul(
                stc("nm4", tt), stc("z3s", tt), -1.0 / D),
                waits=[(VV, z3_used[tt])], inc=SS)
            nm4_done[tt] = S.val(SS)
            emit_tail(tt)
    psB2.close()

    S.emit("sync", lambda sy: sy.nop(),
           waits=[("y_o0", S.val("y_o0")), ("y_o1", S.val("y_o1"))])
    return finish(nc, S)


def finish(nc, S):
    es = ExitStack()
    with es:
        names = set()
        for eng in S.ENGS:
            for waits, fn, inc, n in S.items[eng]:
                if inc is not None:
                    names.add(inc)
                for sname, _ in waits:
                    names.add(sname)
        sems = {}
        for name in sorted(names):
            sems[name] = es.enter_context(nc.semaphore(f"sem_{name}"))
        block = es.enter_context(nc.Block())

        @block.gpsimd
        def _(g):
            S.replay("gpsimd", g, sems)

        @block.tensor
        def _(t):
            S.replay("tensor", t, sems)

        @block.vector
        def _(v):
            S.replay("vector", v, sems)

        @block.scalar
        def _(s_):
            S.replay("scalar", s_, sems)

        @block.sync
        def _(sy):
            S.replay("sync", sy, sems)
    return nc


def make_inputs(full):
    x = np.asarray(full["L0_attn_inputs"], np.float32)
    C = np.ascontiguousarray
    shared = dict(
        wq=C(np.asarray(full["L0_attn_W_Q"], np.float32)),
        wk=C(np.asarray(full["L0_attn_W_K"], np.float32)),
        wv=C(np.asarray(full["L0_attn_W_V"], np.float32)),
        wfc=C(np.asarray(full["L0_attn_W_fc"], np.float32)),
        gam=C(np.broadcast_to(np.asarray(full["L0_attn_gamma"], np.float32), (128, D))),
        bet=C(np.broadcast_to(np.asarray(full["L0_attn_beta"], np.float32), (128, D))),
        wgT=C(np.asarray(full["L0_ffn_Wg"], np.float32).T),
        slsT=C(np.asarray(full["L0_ffn_ln_scale"], np.float32).T),
        lbT=C(np.asarray(full["L0_ffn_ln_bias"], np.float32).T),
        w1=C(np.asarray(full["L0_ffn_W1"], np.float32)),
        b1T=C(np.asarray(full["L0_ffn_b1"], np.float32).T),
        w2=C(np.asarray(full["L0_ffn_W2"], np.float32)),
        b2=C(np.asarray(full["L0_ffn_b2"], np.float32)),
        ident=np.eye(128, dtype=np.float32),
        iota8=C(np.broadcast_to(np.arange(E, dtype=np.float32), (128, E))),
    )
    esel = np.zeros((E, E * 128), np.float32)
    for e in range(E):
        esel[e, e * 128:(e + 1) * 128] = 1.0
    ehead = np.zeros((H, NCH * 128), np.float32)
    for c in range(NCH):
        ehead[c, c * 128:c * 128 + 64] = 1.0
        ehead[8 + c, c * 128 + 64:(c + 1) * 128] = 1.0
    shared["esel"] = esel
    shared["ehead"] = ehead

    in_maps = []
    for core in range(8):
        b, hh = core // 2, core % 2
        sl = x[b, hh * T:(hh + 1) * T, :]
        in_maps.append(dict(shared, x_tm=C(sl), x_fm=C(sl.T)))
    return in_maps


def kernel(**inputs):
    in_maps = make_inputs(inputs)
    nc = build("full")
    res = run_bass_kernel_spmd(nc, in_maps, core_ids=list(range(8)))
    out = np.empty((4, 4096, D), np.float32)
    for core in range(8):
        b, hh = core // 2, core % 2
        out[b, hh * T:(hh + 1) * T, :] = res.results[core]["out"]
    return out


# version 22
# speedup vs baseline: 1.4394x; 1.0066x over previous
import numpy as np
import concourse.bass as bass
import concourse.mybir as mybir
from concourse.bass_utils import run_bass_kernel_spmd
from contextlib import ExitStack

F32 = mybir.dt.float32
BF16 = mybir.dt.bfloat16
U32 = mybir.dt.uint32
AF = mybir.ActivationFunctionType
ALU = mybir.AluOpType

D = 1024
T = 2048
NCH = 8
NT = 16
NN = 4
H = 16
DK = 64
E = 8
HID = 256
EPS = 1e-5
KB = 1024


class Sched:
    ENGS = ("tensor", "vector", "scalar", "gpsimd", "sync")
    OWN = {"vector": "v", "scalar": "s", "gpsimd": "gc"}

    def __init__(self):
        self.items = {e: [] for e in self.ENGS}
        self.cnt = {}

    def val(self, sem):
        return self.cnt.get(sem, 0)

    def emit(self, eng, fn, waits=(), inc=None, n=1):
        waits = list(waits)
        own = self.OWN.get(eng)
        if own is not None and fn is not None:
            prior = self.val(own)
            if prior > 0:
                waits.append((own, prior))
            if inc is None:
                inc = own
        if inc is not None:
            self.cnt[inc] = self.cnt.get(inc, 0) + n
        self.items[eng].append((tuple(waits), fn, inc, n))
        return self.cnt.get(inc, 0) if inc else None

    def barrier(self, *sems):
        return [(s, self.val(s)) for s in sems if self.val(s) > 0]

    def replay(self, eng_name, eng, sems):
        seen = {}
        for waits, fn, inc, n in self.items[eng_name]:
            for s, v in waits:
                if v > 0 and seen.get(s, -1) < v:
                    eng.wait_ge(sems[s], v)
                    seen[s] = v
            if fn is not None:
                inst = fn(eng)
                if inc is not None:
                    inst.then_inc(sems[inc], n)


def build(stage="full", skips=frozenset()):
    nc = bass.Bass("TRN2", target_bir_lowering=False)

    dt = nc.dram_tensor
    x_tm_h = dt("x_tm", [T, D], F32, kind="ExternalInput").ap()
    x_fm_h = dt("x_fm", [D, T], F32, kind="ExternalInput").ap()
    wq_h = dt("wq", [D, D], F32, kind="ExternalInput").ap()
    wk_h = dt("wk", [D, D], F32, kind="ExternalInput").ap()
    wv_h = dt("wv", [D, D], F32, kind="ExternalInput").ap()
    wfc_h = dt("wfc", [D, D], F32, kind="ExternalInput").ap()
    gam_h = dt("gam", [128, D], F32, kind="ExternalInput").ap()
    bet_h = dt("bet", [128, D], F32, kind="ExternalInput").ap()
    wgT_h = dt("wgT", [D, E], F32, kind="ExternalInput").ap()
    slsT_h = dt("slsT", [D, E], F32, kind="ExternalInput").ap()
    lbT_h = dt("lbT", [D, E], F32, kind="ExternalInput").ap()
    w1_h = dt("w1", [E, D, HID], F32, kind="ExternalInput").ap()
    b1T_h = dt("b1T", [HID, E], F32, kind="ExternalInput").ap()
    w2_h = dt("w2", [E, HID, D], F32, kind="ExternalInput").ap()
    b2_h = dt("b2", [E, D], F32, kind="ExternalInput").ap()
    ident_h = dt("ident", [128, 128], F32, kind="ExternalInput").ap()
    iota_h = dt("iota8", [128, E], F32, kind="ExternalInput").ap()
    esel_h = dt("esel", [E, E * 128], F32, kind="ExternalInput").ap()
    ehead_h = dt("ehead", [H, NCH * 128], F32, kind="ExternalInput").ap()

    out_h = dt("out", [T, D], F32, kind="ExternalOutput").ap()
    dbg = {}
    if stage == "kv":
        dbg["kvtot"] = dt("dbg_kvtot", [65, D], F32, kind="ExternalOutput").ap()
    if stage == "fc":
        dbg["fcb"] = dt("dbg_fcb", [128, NT * D], F32, kind="ExternalOutput").ap()
    if stage in ("x1", "b1solo"):
        dbg["x1"] = dt("dbg_x1", [128, NT * D], F32, kind="ExternalOutput").ap()
        dbg["xhat"] = dt("dbg_xhat", [128, NT * D], F32, kind="ExternalOutput").ap()
        dbg["masks"] = dt("dbg_masks", [E, T], F32, kind="ExternalOutput").ap()

    cc_in = dt("cc_in", [65, D], F32).ap()
    cc_out = dt("cc_out", [65, D], F32).ap()
    fcin_h = dt("fcin", [128, NT * D], F32, kind="ExternalInput").ap()         if stage == "b1solo" else None

    SBASE = 16512
    def sb(name, shape, dtp, off):
        end = off + shape[1] * mybir.dt.size(dtp)
        assert end <= 207 * KB, (name, end)
        return nc.alloc_sbuf_tensor_at(name, shape, dtp, offset=SBASE + off)

    o = [0]
    def nxt(sz):
        r = o[0]; o[0] += sz; return r

    ident_f = sb("ident_f", [128, 128], F32, nxt(512))
    iota8 = sb("iota8s", [128, E], F32, nxt(32))
    esel_b = sb("esel_b", [E, E * 128], BF16, nxt(2 * KB))
    ehead_b = sb("ehead_b", [H, NCH * 128], BF16, nxt(2 * KB))
    ones_b = sb("ones_b", [128, 1], BF16, nxt(32))
    wgT_f = sb("wgT_f", [128, NCH * E], F32, nxt(256))
    slsT_f = sb("slsT_f", [128, NCH * E], F32, nxt(256))
    lbT_b = sb("lbT_b", [128, NCH * E], BF16, nxt(128))
    b1T_f = sb("b1T_f", [128, 2 * E], F32, nxt(64))
    b2b = sb("b2b", [E, D], BF16, nxt(2 * KB))
    bprime = sb("bprime", [128, 2 * E], F32, nxt(64))
    st = sb("stats", [128, 64], F32, nxt(256))
    masksfm = sb("masksfm", [E, T], BF16, nxt(4 * KB))
    off_recip = nxt(4 * KB)
    recip_b = sb("recip_b", [H, T], BF16, off_recip)
    z3a = sb("z3a", [128, D], F32, off_recip)
    dtmp = sb("dtmp", [H, 512], F32, nxt(2 * KB))
    lgt = sb("lgt", [128, E], F32, nxt(32))
    mx8 = sb("mx8", [128, E], F32, nxt(32))
    mtm = sb("mtm", [128, E], F32, nxt(32))
    mrs = sb("mrs", [128, 512], BF16, nxt(KB))
    mrsB = sb("mrsB", [128, 512], BF16, nxt(KB))
    wghi = sb("wghi", [128, NCH * E], BF16, nxt(128))
    wglo = sb("wglo", [128, NCH * E], BF16, nxt(128))
    assert o[0] <= 20 * KB, o[0]

    A = 20 * KB
    xTb = sb("xTb", [128, NCH * T], BF16, A)
    W2b = sb("W2b", [128, E * 2 * D], BF16, A)
    Bo = A + 32 * KB
    WQb = sb("WQb", [128, NCH * D], BF16, Bo)
    Qn = sb("Qn", [128, 2 * NCH * 512], BF16, Bo + 16 * KB)
    xhatF = sb("xhatF", [128, NT * D], BF16, Bo)
    Co = Bo + 32 * KB
    WKb = sb("WKb", [128, NCH * D], BF16, Co)
    WVb = sb("WVb", [128, NCH * D], BF16, Co + 16 * KB)
    W1b = sb("W1b", [128, E * NCH * HID], BF16, Co)
    Dr = Co + 32 * KB
    fcb = sb("fcb", [128, NT * D], BF16, Dr)
    x1F = sb("x1F", [128, NT * D], BF16, Dr)
    E1 = Dr + 32 * KB
    Kt = sb("Kt", [128, 2 * D], BF16, E1)
    Vt = sb("Vt", [128, 2 * D], BF16, E1 + 4 * KB)
    kvacc = sb("kvacc", [65, D], F32, E1 + 8 * KB)
    x_tile = sb("x_tile", [128, 2 * D], F32, E1)
    BDf = sb("BDf", [128, D], F32, E1)
    KSf = sb("KSf", [128, 128], F32, E1 + 4 * KB)
    outb = sb("outb", [128, 2 * D], F32, E1 + 8 * KB)
    E2 = E1 + 16 * KB
    BD_b = sb("BD_b", [128, NCH * 128], BF16, E2)
    KS_b = sb("KS_b", [128, NCH * H], BF16, E2 + 2 * KB)
    z_f32 = sb("z_f32", [128, D], F32, E2)
    z2_f32 = sb("z2_f32", [128, D], F32, E2 + 4 * KB)
    Fo = E2 + 8 * KB
    WFCb = sb("WFCb", [128, NCH * D], BF16, Fo)
    gam_r = sb("gam_r", [128, D], F32, Fo)
    bet_r = sb("bet_r", [128, D], F32, Fo + 4 * KB)
    xh_f32 = sb("xh_f32", [128, D], F32, Fo + 8 * KB)
    xhi_b = sb("xhi_b", [128, D], BF16, Fo + 12 * KB)
    xlo_b = sb("xlo_b", [128, D], BF16, Fo + 14 * KB)
    G2 = Fo + 16 * KB
    attn_n = sb("attn_n", [128, 2 * NCH * 512], BF16, G2)
    hall = sb("hall", [128, E * 2 * 512], BF16, G2)
    zB_f = sb("zB_f32", [128, D], F32, G2)
    z2B_f = sb("z2B_f32", [128, D], F32, G2 + 4 * KB)
    xhB_f = sb("xhB_f32", [128, D], F32, G2 + 8 * KB)
    assert G2 + 16 * KB <= 207 * KB

    S = Sched()
    TT, VV, SS, CC = "t", "v", "s", "c"

    def gdma(sem, dst, src, waits=()):
        return S.emit("gpsimd", lambda g, d=dst, s_=src: g.dma_start(out=d, in_=s_),
                      waits=waits, inc=sem, n=16)

    def ydma(sem, dst, src, waits=()):
        return S.emit("sync", lambda sy, d=dst, s_=src: sy.dma_start(out=d, in_=s_),
                      waits=waits, inc=sem, n=16)

    def fm3(h2d, p=128):
        return h2d.rearrange("(c p) n -> p c n", p=p)

    def dst3(hnd, c):
        return hnd[:, :].rearrange("p (c n) -> p c n", c=c)

    VP = type(x_tm_h.ap)

    def mkap(base, ap_list, offset):
        bb = base.copy()
        bb.ap = VP(ap_list)
        bb.offset = offset
        return bb

    if stage != "b1solo":
        gdma("g_wk",
             mkap(WKb[:, :], [[NCH * D, 128], [D, NCH], [1, 512]], 0),
             mkap(wk_h, [[D, 128], [128 * D, NCH], [1, 512]], 0))
        gdma("g_xa",
             mkap(xTb[:, :], [[NCH * T, 128], [T, NCH], [1, 1024]], 0),
             mkap(x_fm_h, [[T, 128], [128 * T, NCH], [1, 1024]], 0))
        gdma("g_wk2",
             mkap(WKb[:, :], [[NCH * D, 128], [D, NCH], [1, 512]], 512),
             mkap(wk_h, [[D, 128], [128 * D, NCH], [1, 512]], 512))
        gdma("g_wv", dst3(WVb, NCH), fm3(wv_h))
        gdma("g_xb",
             mkap(xTb[:, :], [[NCH * T, 128], [T, NCH], [1, 1024]], 1024),
             mkap(x_fm_h, [[T, 128], [128 * T, NCH], [1, 1024]], 1024))
        gdma("g_wq", dst3(WQb, NCH), fm3(wq_h))
        ld_wq = S.val("g_wq")
        gdma("g_wfc", dst3(WFCb, NCH), fm3(wfc_h))
        ld_wfc = S.val("g_wfc")
        gdma("g_eh", ehead_b[:, :], ehead_h[:, :])
        ld_ehead = S.val("g_eh")
    gdma("g_ct", esel_b[:, :], esel_h[:, :])
    gdma("g_ct", dst3(lbT_b, NCH), fm3(lbT_h))
    gdma("g_ct", b2b[:, :], b2_h[:, :])
    ld_consts = S.val("g_ct")

    ydma("y_s1", ident_f[:, :], ident_h[:, :])
    ydma("y_s1", iota8[:, :], iota_h[:, :])
    ydma("y_s1", dst3(wgT_f, NCH), fm3(wgT_h))
    ydma("y_s1", dst3(slsT_f, NCH), fm3(slsT_h))
    ydma("y_s1", dst3(b1T_f, 2), fm3(b1T_h))
    ld_sync1 = S.val("y_s1")

    S.emit("vector", lambda v: v.memset(ones_b[:, :], 1.0), inc=VV)
    ones_done = S.val(VV)
    S.emit("vector", lambda v: v.tensor_copy(wghi[:, :], wgT_f[:, :]),
           waits=[("y_s1", 80)], inc=VV)
    def vWGLO(v):
        with v.bass.allow_low_precision("bf16x2 split of Wg"):
            return v.scalar_tensor_tensor(wglo[:, :], wghi[:, :], -1.0,
                                          wgT_f[:, :], ALU.mult, ALU.add)
    S.emit("vector", vWGLO, inc=VV)
    wg_prep = S.val(VV)
    S.emit("vector", lambda v: v.memset(kvacc[:, :], 0.0), inc=VV)
    kvz = S.val(VV)

    if stage == "b1solo":
        gdma("g_fc", fcb[:, :], fcin_h[:, :])
        ydma("y_gb", gam_r[:, :], gam_h[:, :])
        ydma("y_gb", bet_r[:, :], bet_h[:, :])
        gb_y = S.val("y_gb")
        fcb_evac = [0] * NT
        pipea_t = 0
        num3_t = 0
        w1s_done = 0
        kv_evac = 0
        bd_cast = 0
    else:
        ps1 = ExitStack()
        kp = [ps1.enter_context(nc.psum_tensor(f"kp{i}", [128, D], F32))
              for i in range(2)]
        vp2s = ps1.enter_context(nc.psum_tensor("vp2", [128, D], F32))
        vp2 = [vp2s, vp2s]
        kvps = ps1.enter_context(nc.psum_tensor("kvp", [65, D], F32))
        kvp = [kvps, kvps]

        k_evac = [0] * NT
        v_evac = [0] * NT
        kv_add = [0] * NT
        for tt in range(NT):
            par = tt % 2
            w0 = [("g_wk", 16), ("g_xa", 16)] if tt == 0 else []
            if tt >= 2:
                w0 = [(SS, k_evac[tt - 2])]
            if tt == NT // 2:
                w0 = list(w0) + [("g_xb", 16)]
            for half in range(2):
                for c in range(NCH):
                    hw0 = ()
                    if half == 0 and c == 0:
                        hw0 = w0
                    elif half == 1 and c == 0 and tt == 0:
                        hw0 = [("g_wk2", 16)]
                    def mmK(t, c=c, half=half, tt=tt, par=par):
                        return t.matmul(kp[par][:, half * 512:(half + 1) * 512],
                                        xTb[:, c * T + tt * 128: c * T + tt * 128 + 128],
                                        WKb[:, c * D + half * 512: c * D + half * 512 + 512],
                                        start=(c == 0), stop=(c == NCH - 1))
                    S.emit("tensor", mmK, waits=hw0,
                           inc=TT if (half == 1 and c == NCH - 1) else None)
            k_mm = S.val(TT)
            wv0 = [(VV, v_evac[tt - 1])] if tt >= 1 else [("g_wv", 16)]
            vh_mm = [0, 0]
            for half in range(2):
                for c in range(NCH):
                    def mmV(t, c=c, half=half, tt=tt, par=par):
                        return t.matmul(vp2[par][:, half * 512:(half + 1) * 512],
                                        xTb[:, c * T + tt * 128: c * T + tt * 128 + 128],
                                        WVb[:, c * D + half * 512: c * D + half * 512 + 512],
                                        start=(c == 0), stop=(c == NCH - 1))
                    S.emit("tensor", mmV, waits=wv0 if (half == 0 and c == 0) else (),
                           inc=TT if c == NCH - 1 else None)
                vh_mm[half] = S.val(TT)
            v_mm = S.val(TT)
            S.emit("scalar",
                   lambda s_, par=par: s_.activation(Kt[:, par * D:(par + 1) * D],
                                                     kp[par][:, :], AF.Sigmoid),
                   waits=[(TT, k_mm)], inc=SS)
            k_evac[tt] = S.val(SS)
            S.emit("vector",
                   lambda v, par=par: v.tensor_copy(Vt[:, par * D:par * D + 512],
                                                    vp2[par][:, 0:512]),
                   waits=[(TT, vh_mm[0])], inc=VV)
            S.emit("vector",
                   lambda v, par=par: v.tensor_copy(Vt[:, par * D + 512:(par + 1) * D],
                                                    vp2[par][:, 512:D]),
                   waits=[(TT, vh_mm[1])], inc=VV)
            v_evac[tt] = S.val(VV)
            kvw = [(SS, k_evac[tt]), (VV, max(v_evac[tt], ones_done, kvz))]
            if tt >= 1:
                kvw += [(VV, kv_add[tt - 1])]
            for h in range(H):
                def mmKV(t, h=h, par=par, tt=tt):
                    return t.matmul(kvp[par][0:64, h * DK:(h + 1) * DK],
                                    Kt[:, par * D + h * DK: par * D + (h + 1) * DK],
                                    Vt[:, par * D + h * DK: par * D + (h + 1) * DK],
                                    start=True, stop=True)
                S.emit("tensor", mmKV, waits=kvw if h == 0 else ())
            for half in range(2):
                def mmKS(t, half=half, par=par, tt=tt):
                    return t.matmul(kvp[par][64:65, half * 512:(half + 1) * 512],
                                    ones_b[:, :],
                                    Kt[:, par * D + half * 512: par * D + half * 512 + 512],
                                    start=True, stop=True)
                S.emit("tensor", mmKS, inc=TT if half == 1 else None)
            kv_mm = S.val(TT)
            S.emit("vector",
                   lambda v, par=par: v.tensor_tensor(kvacc[:, :], kvacc[:, :],
                                                      kvp[par][:, :], ALU.add),
                   waits=[(TT, kv_mm)], inc=VV)
            kv_add[tt] = S.val(VV)
        p1_t = S.val(TT)
        kv_evac = kv_add[NT - 1]
        ps1.close()

        gdma("g_pk", cc_in[:, :], kvacc[:, :], waits=[(VV, kv_evac)])
        pack_g = S.val("g_pk")

        def coll(g):
            return g.collective_compute(
                "AllReduce", ALU.add,
                replica_groups=[[0, 1], [2, 3], [4, 5], [6, 7]],
                ins=[cc_in[:, :]], outs=[cc_out[:, :]])
        S.emit("gpsimd", coll, waits=[("g_pk", pack_g)], inc=CC)

        S.emit("vector", lambda v: v.memset(BDf[:, :], 0.0),
               waits=[(TT, p1_t), (SS, k_evac[NT - 1])], inc=VV)
        S.emit("vector", lambda v: v.memset(KSf[:, :], 0.0), inc=VV)
        bdks_zero = S.val(VV)

        for h2 in range(2):
            ydma("y_sc",
                 mkap(BDf[:, :], [[1024, 64], [128, 8], [1, 64]], h2 * 64 * 1024 + h2 * 64),
                 mkap(cc_out[:, :], [[1024, 64], [128, 8], [1, 64]], h2 * 64),
                 waits=[(CC, 1), (VV, bdks_zero)] if h2 == 0 else ())
            S.emit("sync",
                   lambda sy, d=mkap(KSf[:, :], [[128, 64], [17, 8]],
                                     h2 * 64 * 128 + 8 * h2),
                   s_=mkap(cc_out[:, :], [[1, 64], [128, 8]], 64 * 1024 + h2 * 64):
                   sy.dma_start(out=d, in_=s_, allow_slow_non_contiguous=True),
                   inc="y_sc", n=16)
        sc_y = S.val("y_sc")
        S.emit("vector", lambda v: v.tensor_copy(KS_b[:, :], KSf[:, :]),
               waits=[("y_sc", sc_y)], inc=VV)
        ks_cast = S.val(VV)
        S.emit("vector", lambda v: v.tensor_copy(BD_b[:, :], BDf[:, :]), inc=VV)
        bd_cast = S.val(VV)

        if stage == "kv":
            ydma("y_dbg", dbg["kvtot"][0:64, :], cc_out[0:64, :], waits=[(VV, bd_cast)])
            ydma("y_dbg", dbg["kvtot"][64:65, :], cc_out[64:65, :])
            S.emit("sync", lambda sy: sy.nop(), waits=[("y_dbg", S.val("y_dbg"))])
            return finish(nc, S)

        psA = ExitStack()
        qp = [psA.enter_context(nc.psum_tensor(f"qp{i}", [128, 512], F32)) for i in range(2)]
        rbp = [psA.enter_context(nc.psum_tensor(f"rbp{i}", [128, 512], F32)) for i in range(2)]
        nump = [psA.enter_context(nc.psum_tensor(f"nump{i}", [128, 512], F32)) for i in range(2)]
        dnp = rbp[0][0:H, :]
        fcp = psA.enter_context(nc.psum_tensor("fcp", [128, D], F32))

        q_evac = {}
        fcb_evac = [0] * NT

        def emit_Q(n):
            npar = n % 2
            for m in range(NCH):
                idx = n * NCH + m
                w = []
                if idx == 0:
                    w = [(SS, k_evac[NT - 1]), (VV, kv_evac),
                         ("g_wq", ld_wq)]
                if idx >= 2:
                    w += [(SS, q_evac[idx - 2])]
                for c in range(NCH):
                    def mmQ(t, m=m, c=c, n=n):
                        return t.matmul(qp[m % 2][:, :],
                                        WQb[:, c * D + m * 128: c * D + m * 128 + 128],
                                        xTb[:, c * T + n * 512: c * T + n * 512 + 512],
                                        start=(c == 0), stop=(c == NCH - 1))
                    S.emit("tensor", mmQ, waits=w if c == 0 else (),
                           inc=TT if c == NCH - 1 else None)
                qmm = S.val(TT)
                S.emit("scalar",
                       lambda s_, m=m, npar=npar: s_.activation(
                           Qn[:, npar * 4096 + m * 512: npar * 4096 + (m + 1) * 512],
                           qp[m % 2][:, :], AF.Sigmoid),
                       waits=[(TT, qmm)], inc=SS)
                q_evac[idx] = S.val(SS)

        emit_Q(0)
        emit_Q(1)
        attn_prev = 0
        rb_prev = 0
        mrsP = [mrs, mrsB]
        recipd = {}

        def emit_den(n, extra=()):
            npar = n % 2
            dw = [(SS, q_evac[n * NCH + NCH - 1])] + list(extra)
            for c in range(NCH):
                def mmD(t, c=c, npar=npar):
                    return t.matmul(dnp[:, :],
                                    KS_b[:, c * H:(c + 1) * H],
                                    Qn[:, npar * 4096 + c * 512: npar * 4096 + (c + 1) * 512],
                                    start=(c == 0), stop=(c == NCH - 1))
                S.emit("tensor", mmD, waits=dw if c == 0 else (),
                       inc=TT if c == NCH - 1 else None)
            dmm = S.val(TT)
            S.emit("vector", lambda v: v.tensor_scalar(dtmp[:, :], dnp[:, :], 1.0, 1e-8,
                                                       ALU.mult, ALU.add),
                   waits=[(TT, dmm)], inc=VV)
            def vRC(v, n=n):
                with v.bass.allow_low_precision("bf16 matmul operand for rb bcast"):
                    return v.reciprocal(recip_b[:, n * 512:(n + 1) * 512], dtmp[:, :])
            S.emit("vector", vRC, inc=VV)
            recipd[n] = S.val(VV)

        emit_den(0, [(VV, ks_cast)])
        for n in range(NN):
            npar = n % 2
            recip_done = recipd[n]
            rb_evac = {}
            at_evac = {}
            for c in range(NCH):
                w = []
                if c <= 1:
                    w += [(VV, recip_done)]
                    if rb_prev:
                        w += [(SS, rb_prev)]
                    if n == 0 and c == 0:
                        w += [("g_eh", ld_ehead)]
                else:
                    w += [(SS, rb_evac[c - 2])]
                def mmRB(t, c=c, n=n):
                    return t.matmul(rbp[c % 2][:, :], ehead_b[:, c * 128:(c + 1) * 128],
                                    recip_b[:, n * 512:(n + 1) * 512],
                                    start=True, stop=True)
                S.emit("tensor", mmRB, waits=w, inc=TT)
                rbmm = S.val(TT)
                S.emit("scalar", lambda s_, c=c: s_.copy(mrsP[c % 2][:, :], rbp[c % 2][:, :]),
                       waits=[(TT, rbmm), (VV, at_evac.get(c - 2, attn_prev))], inc=SS)
                rb_evac[c] = S.val(SS)
                nw = []
                if c <= 1:
                    nw += [(VV, recip_done)]
                    if n == 0:
                        nw += [(VV, bd_cast)]
                else:
                    nw += [(VV, at_evac[c - 2])]
                def mmNUM(t, c=c, npar=npar):
                    return t.matmul(nump[c % 2][:, :], BD_b[:, c * 128:(c + 1) * 128],
                                    Qn[:, npar * 4096 + c * 512: npar * 4096 + (c + 1) * 512],
                                    start=True, stop=True)
                S.emit("tensor", mmNUM, waits=nw, inc=TT)
                nmm = S.val(TT)
                def vAT(v, c=c, npar=npar):
                    return v.scalar_tensor_tensor(
                        attn_n[:, npar * 4096 + c * 512: npar * 4096 + (c + 1) * 512],
                        nump[c % 2][:, :], 1.0, mrsP[c % 2][:, :], ALU.mult, ALU.mult)
                S.emit("vector", vAT, waits=[(TT, nmm), (SS, rb_evac[c])], inc=VV)
                at_evac[c] = S.val(VV)
            attn_done = at_evac[NCH - 1]
            attn_prev = attn_done
            rb_prev = rb_evac[NCH - 1]
            if n == NN - 1:
                num3_t = S.val(TT)
            if n + 1 < NN:
                emit_den(n + 1, [(SS, rb_evac[NCH - 1])])
            for i in range(4):
                tt = n * 4 + i
                w = [(VV, attn_done)]
                if n == 0 and i == 0:
                    w += [("g_wfc", ld_wfc)]
                if tt >= 1:
                    w += [(SS, fcb_evac[tt - 1])]
                for half in range(2):
                    for c in range(NCH):
                        def mmFC(t, c=c, half=half, i=i, npar=npar):
                            return t.matmul(
                                fcp[:, half * 512:(half + 1) * 512],
                                attn_n[:, npar * 4096 + c * 512 + i * 128:
                                       npar * 4096 + c * 512 + i * 128 + 128],
                                WFCb[:, c * D + half * 512: c * D + half * 512 + 512],
                                start=(c == 0), stop=(c == NCH - 1))
                        S.emit("tensor", mmFC,
                               waits=w if (half == 0 and c == 0) else (),
                               inc=TT if (half == 1 and c == NCH - 1) else None)
                fmm = S.val(TT)
                S.emit("scalar",
                       lambda s_, tt=tt: s_.copy(fcb[:, tt * D:(tt + 1) * D], fcp[:, :]),
                       waits=[(TT, fmm)], inc=SS)
                fcb_evac[tt] = S.val(SS)
            if n + 2 < NN:
                emit_Q(n + 2)
        pipea_t = S.val(TT)
        psA.close()

        gdma("g_w1", W1b[:, :].rearrange("p (e c h) -> p e c h", e=E, c=NCH),
             w1_h.rearrange("e (c p) h -> p e c h", p=128),
             waits=[(TT, p1_t)])
        w1_g = S.val("g_w1")

        ydma("y_gb", gam_r[:, :], gam_h[:, :], waits=[(TT, pipea_t)])
        ydma("y_gb", bet_r[:, :], bet_h[:, :])
        gb_y = S.val("y_gb")

        if stage == "fc":
            gdma("g_dbg", dbg["fcb"][:, :], fcb[:, :], waits=[(SS, fcb_evac[NT - 1])])
            S.emit("gpsimd", lambda g: g.nop(), waits=[("g_dbg", S.val("g_dbg"))])
            return finish(nc, S)


    gdma("g_w2", W2b[:, :].rearrange("p (e ch n) -> p e ch n", e=E, ch=2),
         w2_h.rearrange("e (ch p2) n -> p2 e ch n", p2=128),
         waits=[(TT, pipea_t)])
    w2_g = [S.val("g_w2")]

    psB = ExitStack()
    xppA = psB.enter_context(nc.psum_tensor("xppA", [128, D], F32))
    xppB = psB.enter_context(nc.psum_tensor("xppB", [128, D], F32))
    pb = psB.enter_context(nc.psum_tensor("pb", [128, 256], F32))
    bpp = psB.enter_context(nc.psum_tensor("bpp", [128, 2], F32))
    lgp = pb[:, 0:E]
    mkp = pb[0:E, 128:256]

    b1_bar = S.barrier(SS, VV) + ([(TT, num3_t)] if num3_t else [])
    if stage == "b1solo":
        b1_bar = b1_bar + [("g_fc", 16)]

    COLS = {"zs": 0, "nm1": 1, "t1": 2, "s1": 3, "r1": 4, "m1r1": 5,
            "z2s": 6, "nm2": 7, "a": 8, "sa": 9, "r2": 10, "ra": 11,
            "v2": 12, "v3": 13, "c3": 14, "s3": 15, "r3": 16, "r23": 17,
            "n2r2": 18, "n2r23": 19, "z3s": 20, "nm4": 21, "a4": 22,
            "s4": 23, "r4": 24, "n4r4": 25}

    def stc(name, tt):
        base = (tt % 2) * 32 + COLS[name]
        return st[:, base:base + 1]

    x_loaded = [0] * (NT + 2)
    x_free = [0] * NT
    mask_done = [0] * NT
    sq1_done = [0] * NT
    sq2_done = [0] * NT
    tp_done = [0] * NT
    xh_done = [0] * NT
    xh_val = [0] * NT
    hilo_d = [0] * NT
    mkd_d = [0] * NT
    for tt in range(2):
        xw = [(VV, kv_evac if stage == "b1solo" else max(kv_evac, bd_cast))]
        ydma(f"y_x{tt % 2}", x_tile[:, (tt % 2) * D:(tt % 2 + 1) * D],
             x_tm_h[tt * 128:(tt + 1) * 128, :], waits=xw)
        x_loaded[tt] = S.val(f"y_x{tt % 2}")

    def stretch_gen(tt):
        par = tt % 2
        zb_ = z_f32 if par == 0 else zB_f
        z2_ = z2_f32 if par == 0 else z2B_f
        xh_ = xh_f32 if par == 0 else xhB_f
        xpp = xppA if par == 0 else xppB
        w = [(f"y_x{par}", x_loaded[tt]), (SS, fcb_evac[tt])]
        if tt == 0:
            w += list(b1_bar)
        if tt >= 2:
            w += [(VV, x_free[tt - 2])]

        def vZ(v, tt=tt, par=par, zb_=zb_):
            return v.scalar_tensor_tensor(
                zb_[:, :], fcb[:, tt * D:(tt + 1) * D], 1.0,
                x_tile[:, par * D:(par + 1) * D],
                ALU.mult, ALU.add, accum_out=stc("zs", tt))
        S.emit("vector", vZ, waits=w, inc=VV)
        v1c = S.val(VV)
        S.emit("scalar", lambda s_, tt=tt: s_.mul(stc("nm1", tt), stc("zs", tt),
                                                  -1.0 / D),
               waits=[(VV, v1c)], inc=SS)
        def sSq(s_, tt=tt, zb_=zb_):
            return s_.activation(fcb[:, tt * D:(tt + 1) * D], zb_[:, :],
                                 AF.Square, bias=stc("nm1", tt),
                                 accum_out=stc("t1", tt))
        S.emit("scalar", sSq, inc=SS)
        sq1_done[tt] = S.val(SS)
        S.emit("scalar", lambda s_, tt=tt: s_.activation(
            stc("t1", tt), stc("t1", tt), AF.Copy, scale=1.0 / D, bias=EPS),
            inc=SS)
        S.emit("scalar", lambda s_, tt=tt: s_.activation(
            stc("s1", tt), stc("t1", tt), AF.Sqrt), inc=SS)
        s1d = S.val(SS)
        yield

        S.emit("vector", lambda v, tt=tt: v.reciprocal(stc("r1", tt), stc("s1", tt)),
               waits=[(SS, s1d)], inc=VV)
        S.emit("vector", lambda v, tt=tt: v.tensor_tensor(
            stc("m1r1", tt), stc("nm1", tt), stc("r1", tt), ALU.mult), inc=VV)
        S.emit("vector", lambda v, tt=tt, zb_=zb_: v.tensor_scalar(
            zb_[:, :], zb_[:, :], stc("r1", tt), stc("m1r1", tt),
            ALU.mult, ALU.add), inc=VV)
        S.emit("vector", lambda v, zb_=zb_: v.tensor_tensor(
            zb_[:, :], zb_[:, :], gam_r[:, :], ALU.mult),
            waits=[("y_gb", gb_y)], inc=VV)
        S.emit("vector", lambda v, zb_=zb_: v.tensor_tensor(
            zb_[:, :], zb_[:, :], bet_r[:, :], ALU.add), inc=VV)
        yield

        w2w = []
        if tt >= 2:
            w2w += [(SS, sq2_done[tt - 2]), (VV, xh_val[tt - 2])]
        def vZ2(v, tt=tt, par=par, zb_=zb_, z2_=z2_):
            return v.scalar_tensor_tensor(
                z2_[:, :], zb_[:, :], 1.0,
                x_tile[:, par * D:(par + 1) * D],
                ALU.mult, ALU.add, accum_out=stc("z2s", tt))
        S.emit("vector", vZ2, waits=w2w, inc=VV)
        x_free[tt] = S.val(VV)
        S.emit("scalar", lambda s_, tt=tt: s_.mul(stc("nm2", tt), stc("z2s", tt),
                                                  -1.0 / D),
               waits=[(VV, x_free[tt])], inc=SS)
        if tt + 2 < NT:
            ydma(f"y_x{par}", x_tile[:, par * D:(par + 1) * D],
                 x_tm_h[(tt + 2) * 128:(tt + 3) * 128, :],
                 waits=[(VV, x_free[tt])])
            x_loaded[tt + 2] = S.val(f"y_x{par}")
        def sSq2(s_, tt=tt, z2_=z2_):
            return s_.activation(fcb[:, tt * D:(tt + 1) * D], z2_[:, :],
                                 AF.Square, bias=stc("nm2", tt),
                                 accum_out=stc("a", tt))
        S.emit("scalar", sSq2, inc=SS)
        sq2_done[tt] = S.val(SS)
        yield

        S.emit("scalar", lambda s_, tt=tt: s_.mul(stc("v2", tt), stc("a", tt),
                                                  1.0 / D), inc=SS)
        S.emit("scalar", lambda s_, tt=tt: s_.activation(
            stc("a", tt), stc("v2", tt), AF.Copy, bias=EPS), inc=SS)
        S.emit("scalar", lambda s_, tt=tt: s_.activation(
            stc("sa", tt), stc("a", tt), AF.Sqrt), inc=SS)
        sad = S.val(SS)
        yield

        S.emit("vector", lambda v, tt=tt: v.reciprocal(stc("r2", tt), stc("sa", tt)),
               waits=[(SS, sad)], inc=VV)
        S.emit("vector", lambda v, tt=tt: v.reciprocal(stc("ra", tt), stc("a", tt)),
               inc=VV)
        rac = S.val(VV)
        yield

        S.emit("scalar", lambda s_, tt=tt: s_.activation(
            stc("v3", tt), stc("v2", tt), AF.Copy, scale=stc("ra", tt)),
            waits=[(VV, rac)], inc=SS)
        S.emit("scalar", lambda s_, tt=tt: s_.activation(
            stc("c3", tt), stc("v3", tt), AF.Copy, bias=EPS), inc=SS)
        S.emit("scalar", lambda s_, tt=tt: s_.activation(
            stc("s3", tt), stc("c3", tt), AF.Sqrt), inc=SS)
        s3d = S.val(SS)
        yield

        S.emit("vector", lambda v, tt=tt: v.reciprocal(stc("r3", tt), stc("s3", tt)),
               waits=[(SS, s3d)], inc=VV)
        S.emit("vector", lambda v, tt=tt: v.tensor_tensor(
            stc("r23", tt), stc("r2", tt), stc("r3", tt), ALU.mult), inc=VV)
        S.emit("vector", lambda v, tt=tt: v.tensor_tensor(
            stc("n2r2", tt), stc("nm2", tt), stc("r2", tt), ALU.mult), inc=VV)
        S.emit("vector", lambda v, tt=tt: v.tensor_tensor(
            stc("n2r23", tt), stc("nm2", tt), stc("r23", tt), ALU.mult), inc=VV)
        yield

        def vX1(v, tt=tt, z2_=z2_):
            return v.tensor_scalar(
                x1F[:, tt * D:(tt + 1) * D],
                z2_[:, :], stc("r2", tt), stc("n2r2", tt), ALU.mult, ALU.add)
        S.emit("vector", vX1, inc=VV)

        def vXH(v, tt=tt, z2_=z2_, xh_=xh_):
            return v.tensor_scalar(
                xh_[:, :], z2_[:, :], stc("r23", tt), stc("n2r23", tt),
                ALU.mult, ALU.add)
        S.emit("vector", vXH,
               waits=[(TT, tp_done[tt - 2])] if tt >= 2 else (), inc=VV)
        xh_val[tt] = S.val(VV)
        yield

        tw = [(VV, xh_val[tt]), ("y_s1", ld_sync1)]
        if tt >= 2:
            tw += [(SS, xh_done[tt - 2]), (VV, hilo_d[tt - 2])]
        for c in range(NCH):
            def mmT(t, c=c, xh_=xh_):
                return t.transpose(xpp[:, c * 128:(c + 1) * 128],
                                   xh_[:, c * 128:(c + 1) * 128], ident_f[:, :])
            S.emit("tensor", mmT, waits=tw if c == 0 else (),
                   inc=TT if c == NCH - 1 else None)
        tp_done[tt] = S.val(TT)
        yield

        def vXE(s_, tt=tt):
            q, i = tt // 4, tt % 4
            out_ap = mkap(xhatF[:, :], [[16384, 128], [512, 8], [1, 128]],
                          q * 4096 + i * 128)
            in_ap = xpp[:, :].rearrange("p (c n) -> p c n", c=NCH)
            return s_.copy(out_ap, in_ap)
        S.emit("scalar", vXE, waits=[(TT, tp_done[tt])], inc=SS)
        xh_done[tt] = S.val(SS)
        if "logits" in skips:
            mask_done[tt] = S.val(VV)
            hilo_d[tt] = S.val(VV)
            mkd_d[tt] = tp_done[tt]
            return
        def vLO(v, tt=tt):
            q, i = tt // 4, tt % 4
            hi_ap = mkap(xhatF[:, :], [[16384, 128], [512, 8], [1, 128]],
                         q * 4096 + i * 128)
            with v.bass.allow_low_precision("bf16x2 split"):
                return v.scalar_tensor_tensor(xlo_b[:, :], hi_ap, -1.0,
                                              xpp[:, :], ALU.mult, ALU.add)
        S.emit("vector", vLO, waits=[(SS, xh_done[tt])], inc=VV)
        hilo_d[tt] = S.val(VV)
        yield

        lw = [(VV, max(hilo_d[tt], wg_prep))]
        if tt >= 1:
            lw += [(VV, mask_done[tt - 1])]
        qq, ii = tt // 4, tt % 4
        lg_ops = []
        for c in range(NCH):
            hi_sl = (xhatF, qq * 4096 + c * 512 + ii * 128)
            lg_ops.append((hi_sl, wghi, c))
            lg_ops.append((hi_sl, wglo, c))
            lg_ops.append(((xlo_b, c * 128), wghi, c))
        for i2, ((xb_, xoff), wb_, c) in enumerate(lg_ops):
            def mmLG(t, xb_=xb_, xoff=xoff, wb_=wb_, c=c, i2=i2):
                return t.matmul(lgp, xb_[:, xoff:xoff + 128],
                                wb_[:, c * E:(c + 1) * E],
                                start=(i2 == 0), stop=(i2 == len(lg_ops) - 1))
            S.emit("tensor", mmLG, waits=lw if i2 == 0 else (),
                   inc=TT if i2 == len(lg_ops) - 1 else None)
        lgd = S.val(TT)
        S.emit("vector", lambda v: v.tensor_copy(lgt[:, :], lgp),
               waits=[(TT, lgd)], inc=VV)
        S.emit("vector", lambda v: v.reduce_max(
            mx8[:, 0:1], lgt[:, :], axis=mybir.AxisListType.X), inc=VV)
        mxd = S.val(VV)
        eqw = [(VV, mxd)]
        if tt >= 1:
            eqw += [(TT, mkd_d[tt - 1])]
        def vEQ(v):
            with v.bass.allow_low_precision("0/1 mask is exact in bf16"):
                return v.tensor_scalar(
                    mtm[:, :], lgt[:, :], mx8[:, 0:1], None, ALU.is_equal)
        S.emit("vector", vEQ, waits=eqw, inc=VV)
        mtmd = S.val(VV)
        mtw = [(VV, mtmd)]
        if tt >= 1:
            mtw += [(VV, mask_done[tt - 1])]
        S.emit("tensor", lambda t: t.transpose(mkp, mtm[:, :], ident_f[:, :]),
               waits=mtw, inc=TT)
        mkd_d[tt] = S.val(TT)
        S.emit("vector", lambda v, tt=tt: v.tensor_copy(
            masksfm[:, tt * 128:(tt + 1) * 128], mkp),
            waits=[(TT, mkd_d[tt])], inc=VV)
        mask_done[tt] = S.val(VV)

    w1s_done = 0
    bp_prev = [0, 0]
    def emit_w1prep(e):
        for ch in range(2):
            w = []
            if e == 0 and ch == 0:
                w = [("g_w1", w1_g), ("g_ct", ld_consts),
                     (SS, fcb_evac[NT - 1])]
            if bp_prev[ch]:
                w = list(w) + [(VV, bp_prev[ch])]
            for c in range(NCH):
                def mmBP(t, e=e, ch=ch, c=c):
                    col = (e * NCH + c) * HID + ch * 128
                    return t.matmul(bpp[:, ch:ch + 1],
                                    W1b[:, col:col + 128],
                                    lbT_b[:, c * E + e: c * E + e + 1],
                                    start=(c == 0), stop=(c == NCH - 1))
                S.emit("tensor", mmBP, waits=w if c == 0 else (),
                       inc=TT if c == NCH - 1 else None)
            bmm = S.val(TT)
            S.emit("vector",
                   lambda v, e=e, ch=ch: v.tensor_tensor(
                       bprime[:, e * 2 + ch: e * 2 + ch + 1],
                       bpp[:, ch:ch + 1], b1T_f[:, ch * E + e: ch * E + e + 1],
                       ALU.add),
                   waits=[(TT, bmm), ("y_s1", ld_sync1)], inc=VV)
            bp_prev[ch] = S.val(VV)
        bmm2 = S.val(TT)
        for c in range(NCH):
            def sSC(s_, e=e, c=c):
                col = (e * NCH + c) * HID
                return s_.activation(W1b[:, col:col + HID], W1b[:, col:col + HID],
                                     AF.Copy, scale=slsT_f[:, c * E + e: c * E + e + 1])
            S.emit("scalar", sSC, waits=[(TT, bmm2)] if c == 0 else (),
                   inc=SS if c == NCH - 1 else None)
        return S.val(SS)

    def drive(g):
        try:
            next(g)
        except StopIteration:
            pass

    for k in range(0, NT, 2):
        ga = stretch_gen(k)
        gb = stretch_gen(k + 1)
        drive(ga); drive(gb)
        if stage != "b1solo" and "w1prep" not in skips and k // 2 < E:
            w1s_done = emit_w1prep(k // 2)
        for who in (ga, gb, ga, gb, ga, ga, ga, ga, ga, gb, gb, gb, gb, gb,
                    ga, gb, ga, ga, gb, gb):
            drive(who)
        for g in (ga, gb):
            for _ in range(8):
                drive(g)

    b1_v = S.val(VV)
    b1_s = S.val(SS)
    psB.close()

    if stage in ("x1", "b1solo"):
        gdma("g_dbg", dbg["x1"][:, :], x1F[:, :], waits=[(VV, b1_v), (SS, b1_s)])
        gdma("g_dbg", dbg["xhat"][:, :], xhatF[:, :])
        gdma("g_dbg", dbg["masks"][:, :], masksfm[:, :])
        S.emit("gpsimd", lambda g: g.nop(), waits=[("g_dbg", S.val("g_dbg"))])
        return finish(nc, S)

    psB2 = ExitStack()
    hp = [psB2.enter_context(nc.psum_tensor(f"hp{i}", [128, 512], F32))
          for i in range(4)]
    yp = [psB2.enter_context(nc.psum_tensor(f"yp{i}", [128, D], F32))
          for i in range(2)]

    b2_bar = [(VV, b1_v), (SS, b1_s)]

    z3_used = [0] * NT
    sq4_done = [0] * NT
    nm4_done = [0] * NT
    tail_done = {}
    out_dma = [0] * NT

    def emit_tail(tt):
        zb = z3a if tt % 2 == 0 else xh_f32
        def sSq4(s_, tt=tt, zb=zb):
            return s_.activation(x1F[:, tt * D:(tt + 1) * D], zb[:, :],
                                 AF.Square, bias=stc("nm4", tt),
                                 accum_out=stc("a4", tt))
        S.emit("scalar", sSq4,
               waits=[(VV, z3_used[tt]), (SS, nm4_done[tt])], inc=SS)
        sq4c = S.val(SS)
        S.emit("scalar", lambda s_, tt=tt: s_.activation(
            stc("a4", tt), stc("a4", tt), AF.Copy, bias=EPS, scale=1.0 / D),
            waits=[(SS, sq4c)], inc=SS)
        S.emit("scalar", lambda s_, tt=tt: s_.activation(
            stc("s4", tt), stc("a4", tt), AF.Sqrt), inc=SS)
        sq4_done[tt] = S.val(SS)
        s4d = S.val(SS)
        S.emit("vector", lambda v, tt=tt: v.reciprocal(stc("r4", tt), stc("s4", tt)),
               waits=[(SS, s4d)], inc=VV)
        r4d = S.val(VV)
        S.emit("scalar", lambda s_, tt=tt: s_.activation(
            stc("n4r4", tt), stc("nm4", tt), AF.Copy, scale=stc("r4", tt)),
            waits=[(VV, r4d)], inc=SS)
        n4d = S.val(SS)
        ow = [(SS, n4d)]
        if tt >= 2:
            ow += [(f"y_o{tt % 2}", out_dma[tt - 2])]
        def vOUT(v, tt=tt, zb=zb, par=tt % 2):
            return v.tensor_scalar(
                outb[:, par * D:(par + 1) * D], zb[:, :],
                stc("r4", tt), stc("n4r4", tt), ALU.mult, ALU.add)
        S.emit("vector", vOUT, waits=ow + [(VV, r4d)], inc=VV)
        od = S.val(VV)
        tail_done[tt] = od
        ydma(f"y_o{tt % 2}", out_h[tt * 128:(tt + 1) * 128, :],
             outb[:, (tt % 2) * D:(tt % 2 + 1) * D], waits=[(VV, od)])
        out_dma[tt] = S.val(f"y_o{tt % 2}")

    for q in range(NN):
        gelu_d = {}
        msk_d = {}
        mrep_d = {}
        for e in range(E):
            epar = e % 2
            w = [(VV, mask_done[q * 4 + 3])]
            if q == 0 and e == 0:
                w += list(b2_bar)
            if e >= 2:
                w += [(SS, mrep_d[e - 2])]
            elif q >= 1:
                w += [(VV, z3_used[q * 4 - 2])]
            def mmMR(t, e=e, q=q, epar=epar):
                return t.matmul(yp[0][:, epar * 512:(epar + 1) * 512],
                                esel_b[:, e * 128:(e + 1) * 128],
                                masksfm[:, q * 512:(q + 1) * 512],
                                start=True, stop=True)
            S.emit("tensor", mmMR, waits=w, inc=TT)
            mrd = S.val(TT)
            S.emit("scalar", lambda s_, epar=epar: s_.copy(
                mrsP[epar][:, :], yp[0][:, epar * 512:(epar + 1) * 512]),
                   waits=[(TT, mrd), (VV, msk_d.get(e - 2, 0))], inc=SS)
            mrep_d[e] = S.val(SS)
            hw = [(SS, max(xh_done[q * 4 + 3], w1s_done))]
            if e >= 2:
                hw += [(SS, gelu_d[e - 2])]
            for ch in range(2):
                for c in range(NCH):
                    def mmH(t, e=e, ch=ch, c=c, q=q, epar=epar):
                        col = (e * NCH + c) * HID + ch * 128
                        return t.matmul(
                            hp[epar * 2 + ch][:, :],
                            W1b[:, col:col + 128],
                            xhatF[:, q * 4096 + c * 512: q * 4096 + (c + 1) * 512],
                            start=(c == 0), stop=(c == NCH - 1))
                    S.emit("tensor", mmH,
                           waits=hw if (ch == 0 and c == 0) else (),
                           inc=TT if (ch == 1 and c == NCH - 1) else None)
            hmm = S.val(TT)
            for ch in range(2):
                def sGE(s_, e=e, ch=ch, epar=epar):
                    return s_.activation(
                        hall[:, (e * 2 + ch) * 512:(e * 2 + ch + 1) * 512],
                        hp[epar * 2 + ch][:, :], AF.Gelu,
                        bias=bprime[:, e * 2 + ch:e * 2 + ch + 1])
                S.emit("scalar", sGE, waits=[(TT, hmm)] if ch == 0 else (),
                       inc=SS if ch == 1 else None)
            gelu_d[e] = S.val(SS)
            for ch in range(2):
                def vMM(v, e=e, ch=ch, epar=epar):
                    sl = hall[:, (e * 2 + ch) * 512:(e * 2 + ch + 1) * 512]
                    return v.tensor_tensor(sl, sl, mrsP[epar][:, :], ALU.mult)
                S.emit("vector", vMM,
                       waits=[(SS, max(gelu_d[e], mrep_d[e]))] if ch == 0 else (),
                       inc=VV if ch == 1 else None)
            msk_d[e] = S.val(VV)

        for i in range(4):
            tt = q * 4 + i
            ypt = yp[tt % 2]
            w = [(VV, msk_d[E - 1]), ("g_w2", max(w2_g)), ("g_ct", ld_consts),
                 (SS, mrep_d[E - 1])]
            if tt >= 2:
                w += [(VV, z3_used[tt - 2])]
            for nh in range(2):
                def mmB2(t, tt=tt, nh=nh, ypt=ypt):
                    return t.matmul(ypt[:, nh * 512:(nh + 1) * 512],
                                    masksfm[:, tt * 128:(tt + 1) * 128],
                                    b2b[:, nh * 512:(nh + 1) * 512],
                                    start=True, stop=False)
                S.emit("tensor", mmB2, waits=w if nh == 0 else ())
            for e in range(E):
                for ch in range(2):
                    for nh in range(2):
                        def mmY(t, e=e, ch=ch, nh=nh, i=i, ypt=ypt):
                            hcol = (e * 2 + ch) * 512 + i * 128
                            wcol = (e * 2 + ch) * D + nh * 512
                            return t.matmul(
                                ypt[:, nh * 512:(nh + 1) * 512],
                                hall[:, hcol:hcol + 128],
                                W2b[:, wcol:wcol + 512],
                                start=False,
                                stop=(e == E - 1 and ch == 1))
                        S.emit("tensor", mmY,
                               inc=TT if (e == E - 1 and ch == 1 and nh == 1) else None)
            ymm = S.val(TT)
            zb = z3a if tt % 2 == 0 else xh_f32
            vz3w = [(TT, ymm)]
            if tt >= 2:
                vz3w += [(SS, sq4_done[tt - 2])]
            def vZ3(v, tt=tt, zb=zb, ypt=ypt):
                return v.scalar_tensor_tensor(
                    zb[:, :], ypt[:, :], 1.0,
                    x1F[:, tt * D:(tt + 1) * D],
                    ALU.mult, ALU.add, accum_out=stc("z3s", tt))
            S.emit("vector", vZ3, waits=vz3w, inc=VV)
            z3_used[tt] = S.val(VV)
            S.emit("scalar", lambda s_, tt=tt: s_.mul(
                stc("nm4", tt), stc("z3s", tt), -1.0 / D),
                waits=[(VV, z3_used[tt])], inc=SS)
            nm4_done[tt] = S.val(SS)
            emit_tail(tt)
    psB2.close()

    S.emit("sync", lambda sy: sy.nop(),
           waits=[("y_o0", S.val("y_o0")), ("y_o1", S.val("y_o1"))])
    return finish(nc, S)


def finish(nc, S):
    es = ExitStack()
    with es:
        names = set()
        for eng in S.ENGS:
            for waits, fn, inc, n in S.items[eng]:
                if inc is not None:
                    names.add(inc)
                for sname, _ in waits:
                    names.add(sname)
        sems = {}
        for name in sorted(names):
            sems[name] = es.enter_context(nc.semaphore(f"sem_{name}"))
        block = es.enter_context(nc.Block())

        @block.gpsimd
        def _(g):
            S.replay("gpsimd", g, sems)

        @block.tensor
        def _(t):
            S.replay("tensor", t, sems)

        @block.vector
        def _(v):
            S.replay("vector", v, sems)

        @block.scalar
        def _(s_):
            S.replay("scalar", s_, sems)

        @block.sync
        def _(sy):
            S.replay("sync", sy, sems)
    return nc


def make_inputs(full):
    x = np.asarray(full["L0_attn_inputs"], np.float32)
    C = np.ascontiguousarray
    shared = dict(
        wq=C(np.asarray(full["L0_attn_W_Q"], np.float32)),
        wk=C(np.asarray(full["L0_attn_W_K"], np.float32)),
        wv=C(np.asarray(full["L0_attn_W_V"], np.float32)),
        wfc=C(np.asarray(full["L0_attn_W_fc"], np.float32)),
        gam=C(np.broadcast_to(np.asarray(full["L0_attn_gamma"], np.float32), (128, D))),
        bet=C(np.broadcast_to(np.asarray(full["L0_attn_beta"], np.float32), (128, D))),
        wgT=C(np.asarray(full["L0_ffn_Wg"], np.float32).T),
        slsT=C(np.asarray(full["L0_ffn_ln_scale"], np.float32).T),
        lbT=C(np.asarray(full["L0_ffn_ln_bias"], np.float32).T),
        w1=C(np.asarray(full["L0_ffn_W1"], np.float32)),
        b1T=C(np.asarray(full["L0_ffn_b1"], np.float32).T),
        w2=C(np.asarray(full["L0_ffn_W2"], np.float32)),
        b2=C(np.asarray(full["L0_ffn_b2"], np.float32)),
        ident=np.eye(128, dtype=np.float32),
        iota8=C(np.broadcast_to(np.arange(E, dtype=np.float32), (128, E))),
    )
    esel = np.zeros((E, E * 128), np.float32)
    for e in range(E):
        esel[e, e * 128:(e + 1) * 128] = 1.0
    ehead = np.zeros((H, NCH * 128), np.float32)
    for c in range(NCH):
        ehead[c, c * 128:c * 128 + 64] = 1.0
        ehead[8 + c, c * 128 + 64:(c + 1) * 128] = 1.0
    shared["esel"] = esel
    shared["ehead"] = ehead

    in_maps = []
    for core in range(8):
        b, hh = core // 2, core % 2
        sl = x[b, hh * T:(hh + 1) * T, :]
        in_maps.append(dict(shared, x_tm=C(sl), x_fm=C(sl.T)))
    return in_maps


def kernel(**inputs):
    in_maps = make_inputs(inputs)
    nc = build("full")
    res = run_bass_kernel_spmd(nc, in_maps, core_ids=list(range(8)))
    out = np.empty((4, 4096, D), np.float32)
    for core in range(8):
        b, hh = core // 2, core % 2
        out[b, hh * T:(hh + 1) * T, :] = res.results[core]["out"]
    return out


# version 23 (final)
# speedup vs baseline: 1.4523x; 1.0090x over previous
import numpy as np
import concourse.bass as bass
import concourse.mybir as mybir
from concourse.bass_utils import run_bass_kernel_spmd
from contextlib import ExitStack

F32 = mybir.dt.float32
BF16 = mybir.dt.bfloat16
U32 = mybir.dt.uint32
AF = mybir.ActivationFunctionType
ALU = mybir.AluOpType

D = 1024
T = 2048
NCH = 8
NT = 16
NN = 4
H = 16
DK = 64
E = 8
HID = 256
EPS = 1e-5
KB = 1024


class Sched:
    ENGS = ("tensor", "vector", "scalar", "gpsimd", "sync")
    OWN = {"vector": "v", "scalar": "s", "gpsimd": "gc"}

    def __init__(self):
        self.items = {e: [] for e in self.ENGS}
        self.cnt = {}

    def val(self, sem):
        return self.cnt.get(sem, 0)

    def emit(self, eng, fn, waits=(), inc=None, n=1):
        waits = list(waits)
        own = self.OWN.get(eng)
        if own is not None and fn is not None:
            prior = self.val(own)
            if prior > 0:
                waits.append((own, prior))
            if inc is None:
                inc = own
        if inc is not None:
            self.cnt[inc] = self.cnt.get(inc, 0) + n
        self.items[eng].append((tuple(waits), fn, inc, n))
        return self.cnt.get(inc, 0) if inc else None

    def barrier(self, *sems):
        return [(s, self.val(s)) for s in sems if self.val(s) > 0]

    def replay(self, eng_name, eng, sems):
        seen = {}
        for waits, fn, inc, n in self.items[eng_name]:
            for s, v in waits:
                if v > 0 and seen.get(s, -1) < v:
                    eng.wait_ge(sems[s], v)
                    seen[s] = v
            if fn is not None:
                inst = fn(eng)
                if inc is not None:
                    inst.then_inc(sems[inc], n)


def build(stage="full", skips=frozenset()):
    nc = bass.Bass("TRN2", target_bir_lowering=False)

    dt = nc.dram_tensor
    x_tm_h = dt("x_tm", [T, D], F32, kind="ExternalInput").ap()
    x_fm_h = dt("x_fm", [D, T], F32, kind="ExternalInput").ap()
    wq_h = dt("wq", [D, D], F32, kind="ExternalInput").ap()
    wk_h = dt("wk", [D, D], F32, kind="ExternalInput").ap()
    wv_h = dt("wv", [D, D], F32, kind="ExternalInput").ap()
    wfc_h = dt("wfc", [D, D], F32, kind="ExternalInput").ap()
    gam_h = dt("gam", [128, D], F32, kind="ExternalInput").ap()
    bet_h = dt("bet", [128, D], F32, kind="ExternalInput").ap()
    wgT_h = dt("wgT", [D, E], F32, kind="ExternalInput").ap()
    slsT_h = dt("slsT", [D, E], F32, kind="ExternalInput").ap()
    lbT_h = dt("lbT", [D, E], F32, kind="ExternalInput").ap()
    w1_h = dt("w1", [E, D, HID], F32, kind="ExternalInput").ap()
    b1T_h = dt("b1T", [HID, E], F32, kind="ExternalInput").ap()
    w2_h = dt("w2", [E, HID, D], F32, kind="ExternalInput").ap()
    b2_h = dt("b2", [E, D], F32, kind="ExternalInput").ap()
    ident_h = dt("ident", [128, 128], F32, kind="ExternalInput").ap()
    iota_h = dt("iota8", [128, E], F32, kind="ExternalInput").ap()
    esel_h = dt("esel", [E, E * 128], F32, kind="ExternalInput").ap()
    ehead_h = dt("ehead", [H, NCH * 128], F32, kind="ExternalInput").ap()

    out_h = dt("out", [T, D], F32, kind="ExternalOutput").ap()
    dbg = {}
    if stage == "kv":
        dbg["kvtot"] = dt("dbg_kvtot", [65, D], F32, kind="ExternalOutput").ap()
    if stage == "fc":
        dbg["fcb"] = dt("dbg_fcb", [128, NT * D], F32, kind="ExternalOutput").ap()
    if stage in ("x1", "b1solo"):
        dbg["x1"] = dt("dbg_x1", [128, NT * D], F32, kind="ExternalOutput").ap()
        dbg["xhat"] = dt("dbg_xhat", [128, NT * D], F32, kind="ExternalOutput").ap()
        dbg["masks"] = dt("dbg_masks", [E, T], F32, kind="ExternalOutput").ap()

    cc_in = dt("cc_in", [65, D], F32).ap()
    cc_out = dt("cc_out", [65, D], F32).ap()
    fcin_h = dt("fcin", [128, NT * D], F32, kind="ExternalInput").ap()         if stage == "b1solo" else None

    SBASE = 16512
    def sb(name, shape, dtp, off):
        end = off + shape[1] * mybir.dt.size(dtp)
        assert end <= 207 * KB, (name, end)
        return nc.alloc_sbuf_tensor_at(name, shape, dtp, offset=SBASE + off)

    o = [0]
    def nxt(sz):
        r = o[0]; o[0] += sz; return r

    ident_f = sb("ident_f", [128, 128], F32, nxt(512))
    iota8 = sb("iota8s", [128, E], F32, nxt(32))
    esel_b = sb("esel_b", [E, E * 128], BF16, nxt(2 * KB))
    ehead_b = sb("ehead_b", [H, NCH * 128], BF16, nxt(2 * KB))
    ones_b = sb("ones_b", [128, 1], BF16, nxt(32))
    wgT_f = sb("wgT_f", [128, NCH * E], F32, nxt(256))
    slsT_f = sb("slsT_f", [128, NCH * E], F32, nxt(256))
    lbT_b = sb("lbT_b", [128, NCH * E], BF16, nxt(128))
    b1T_f = sb("b1T_f", [128, 2 * E], F32, nxt(64))
    b2b = sb("b2b", [E, D], BF16, nxt(2 * KB))
    bprime = sb("bprime", [128, 2 * E], F32, nxt(64))
    st = sb("stats", [128, 64], F32, nxt(256))
    masksfm = sb("masksfm", [E, T], BF16, nxt(4 * KB))
    off_recip = nxt(4 * KB)
    recip_b = sb("recip_b", [H, T], BF16, off_recip)
    z3a = sb("z3a", [128, D], F32, off_recip)
    dtmp = sb("dtmp", [H, 512], F32, nxt(2 * KB))
    lgt = sb("lgt", [128, E], F32, nxt(32))
    mx8 = sb("mx8", [128, E], F32, nxt(32))
    mtm = sb("mtm", [128, E], F32, nxt(32))
    mrs = sb("mrs", [128, 512], BF16, nxt(KB))
    mrsB = sb("mrsB", [128, 512], BF16, nxt(KB))
    wghi = sb("wghi", [128, NCH * E], BF16, nxt(128))
    wglo = sb("wglo", [128, NCH * E], BF16, nxt(128))
    assert o[0] <= 20 * KB, o[0]

    A = 20 * KB
    xTb = sb("xTb", [128, NCH * T], BF16, A)
    W2b = sb("W2b", [128, E * 2 * D], BF16, A)
    Bo = A + 32 * KB
    WQb = sb("WQb", [128, NCH * D], BF16, Bo)
    Qn = sb("Qn", [128, 2 * NCH * 512], BF16, Bo + 16 * KB)
    xhatF = sb("xhatF", [128, NT * D], BF16, Bo)
    Co = Bo + 32 * KB
    WKb = sb("WKb", [128, NCH * D], BF16, Co)
    WVb = sb("WVb", [128, NCH * D], BF16, Co + 16 * KB)
    W1b = sb("W1b", [128, E * NCH * HID], BF16, Co)
    Dr = Co + 32 * KB
    fcb = sb("fcb", [128, NT * D], BF16, Dr)
    x1F = sb("x1F", [128, NT * D], BF16, Dr)
    E1 = Dr + 32 * KB
    Kt = sb("Kt", [128, 2 * D], BF16, E1)
    Vt = sb("Vt", [128, 2 * D], BF16, E1 + 4 * KB)
    kvacc = sb("kvacc", [65, D], F32, E1 + 8 * KB)
    x_tile = sb("x_tile", [128, 2 * D], F32, E1)
    BDf = sb("BDf", [128, D], F32, E1)
    KSf = sb("KSf", [128, 128], F32, E1 + 4 * KB)
    outb = sb("outb", [128, 2 * D], F32, E1 + 8 * KB)
    E2 = E1 + 16 * KB
    BD_b = sb("BD_b", [128, NCH * 128], BF16, E2)
    KS_b = sb("KS_b", [128, NCH * H], BF16, E2 + 2 * KB)
    z_f32 = sb("z_f32", [128, D], F32, E2)
    z2_f32 = sb("z2_f32", [128, D], F32, E2 + 4 * KB)
    Fo = E2 + 8 * KB
    WFCb = sb("WFCb", [128, NCH * D], BF16, Fo)
    gam_r = sb("gam_r", [128, D], F32, Fo)
    bet_r = sb("bet_r", [128, D], F32, Fo + 4 * KB)
    xh_f32 = sb("xh_f32", [128, D], F32, Fo + 8 * KB)
    xhi_b = sb("xhi_b", [128, D], BF16, Fo + 12 * KB)
    xlo_b = sb("xlo_b", [128, D], BF16, Fo + 14 * KB)
    G2 = Fo + 16 * KB
    attn_n = sb("attn_n", [128, 2 * NCH * 512], BF16, G2)
    hall = sb("hall", [128, E * 2 * 512], BF16, G2)
    zB_f = sb("zB_f32", [128, D], F32, G2)
    z2B_f = sb("z2B_f32", [128, D], F32, G2 + 4 * KB)
    xhB_f = sb("xhB_f32", [128, D], F32, G2 + 8 * KB)
    assert G2 + 16 * KB <= 207 * KB

    S = Sched()
    TT, VV, SS, CC = "t", "v", "s", "c"

    def gdma(sem, dst, src, waits=()):
        return S.emit("gpsimd", lambda g, d=dst, s_=src: g.dma_start(out=d, in_=s_),
                      waits=waits, inc=sem, n=16)

    def ydma(sem, dst, src, waits=()):
        return S.emit("sync", lambda sy, d=dst, s_=src: sy.dma_start(out=d, in_=s_),
                      waits=waits, inc=sem, n=16)

    def fm3(h2d, p=128):
        return h2d.rearrange("(c p) n -> p c n", p=p)

    def dst3(hnd, c):
        return hnd[:, :].rearrange("p (c n) -> p c n", c=c)

    VP = type(x_tm_h.ap)

    def mkap(base, ap_list, offset):
        bb = base.copy()
        bb.ap = VP(ap_list)
        bb.offset = offset
        return bb

    if stage != "b1solo":
        gdma("g_wk",
             mkap(WKb[:, :], [[NCH * D, 128], [D, NCH], [1, 512]], 0),
             mkap(wk_h, [[D, 128], [128 * D, NCH], [1, 512]], 0))
        gdma("g_xa",
             mkap(xTb[:, :], [[NCH * T, 128], [T, NCH], [1, 1024]], 0),
             mkap(x_fm_h, [[T, 128], [128 * T, NCH], [1, 1024]], 0))
        gdma("g_wk2",
             mkap(WKb[:, :], [[NCH * D, 128], [D, NCH], [1, 512]], 512),
             mkap(wk_h, [[D, 128], [128 * D, NCH], [1, 512]], 512))
        gdma("g_wv", dst3(WVb, NCH), fm3(wv_h))
        gdma("g_xb",
             mkap(xTb[:, :], [[NCH * T, 128], [T, NCH], [1, 1024]], 1024),
             mkap(x_fm_h, [[T, 128], [128 * T, NCH], [1, 1024]], 1024))
        gdma("g_wq", dst3(WQb, NCH), fm3(wq_h))
        ld_wq = S.val("g_wq")
        gdma("g_wfc", dst3(WFCb, NCH), fm3(wfc_h))
        ld_wfc = S.val("g_wfc")
        gdma("g_eh", ehead_b[:, :], ehead_h[:, :])
        ld_ehead = S.val("g_eh")
    gdma("g_ct", esel_b[:, :], esel_h[:, :])
    gdma("g_ct", dst3(lbT_b, NCH), fm3(lbT_h))
    gdma("g_ct", b2b[:, :], b2_h[:, :])
    ld_consts = S.val("g_ct")

    ydma("y_s1", ident_f[:, :], ident_h[:, :])
    ydma("y_s1", iota8[:, :], iota_h[:, :])
    ydma("y_s1", dst3(wgT_f, NCH), fm3(wgT_h))
    ydma("y_s1", dst3(slsT_f, NCH), fm3(slsT_h))
    ydma("y_s1", dst3(b1T_f, 2), fm3(b1T_h))
    ld_sync1 = S.val("y_s1")

    S.emit("vector", lambda v: v.memset(ones_b[:, :], 1.0), inc=VV)
    ones_done = S.val(VV)
    S.emit("vector", lambda v: v.tensor_copy(wghi[:, :], wgT_f[:, :]),
           waits=[("y_s1", 80)], inc=VV)
    def vWGLO(v):
        with v.bass.allow_low_precision("bf16x2 split of Wg"):
            return v.scalar_tensor_tensor(wglo[:, :], wghi[:, :], -1.0,
                                          wgT_f[:, :], ALU.mult, ALU.add)
    S.emit("vector", vWGLO, inc=VV)
    wg_prep = S.val(VV)
    S.emit("vector", lambda v: v.memset(kvacc[:, :], 0.0), inc=VV)
    kvz = S.val(VV)

    if stage == "b1solo":
        gdma("g_fc", fcb[:, :], fcin_h[:, :])
        ydma("y_gb", gam_r[:, :], gam_h[:, :])
        ydma("y_gb", bet_r[:, :], bet_h[:, :])
        gb_y = S.val("y_gb")
        fcb_evac = [0] * NT
        pipea_t = 0
        num3_t = 0
        w1s_done = 0
        kv_evac = 0
        bd_cast = 0
    else:
        ps1 = ExitStack()
        kp = [ps1.enter_context(nc.psum_tensor(f"kp{i}", [128, D], F32))
              for i in range(2)]
        vp2s = ps1.enter_context(nc.psum_tensor("vp2", [128, D], F32))
        vp2 = [vp2s, vp2s]
        kvps = ps1.enter_context(nc.psum_tensor("kvp", [65, D], F32))
        kvp = [kvps, kvps]

        k_evac = [0] * NT
        v_evac = [0] * NT
        kv_add = [0] * NT
        for tt in range(NT):
            par = tt % 2
            w0 = [("g_wk", 16), ("g_xa", 16)] if tt == 0 else []
            if tt >= 2:
                w0 = [(SS, k_evac[tt - 2])]
            if tt == NT // 2:
                w0 = list(w0) + [("g_xb", 16)]
            for half in range(2):
                for c in range(NCH):
                    hw0 = ()
                    if half == 0 and c == 0:
                        hw0 = w0
                    elif half == 1 and c == 0 and tt == 0:
                        hw0 = [("g_wk2", 16)]
                    def mmK(t, c=c, half=half, tt=tt, par=par):
                        return t.matmul(kp[par][:, half * 512:(half + 1) * 512],
                                        xTb[:, c * T + tt * 128: c * T + tt * 128 + 128],
                                        WKb[:, c * D + half * 512: c * D + half * 512 + 512],
                                        start=(c == 0), stop=(c == NCH - 1))
                    S.emit("tensor", mmK, waits=hw0,
                           inc=TT if (half == 1 and c == NCH - 1) else None)
            k_mm = S.val(TT)
            wv0 = [(VV, v_evac[tt - 1])] if tt >= 1 else [("g_wv", 16)]
            vh_mm = [0, 0]
            for half in range(2):
                for c in range(NCH):
                    def mmV(t, c=c, half=half, tt=tt, par=par):
                        return t.matmul(vp2[par][:, half * 512:(half + 1) * 512],
                                        xTb[:, c * T + tt * 128: c * T + tt * 128 + 128],
                                        WVb[:, c * D + half * 512: c * D + half * 512 + 512],
                                        start=(c == 0), stop=(c == NCH - 1))
                    S.emit("tensor", mmV, waits=wv0 if (half == 0 and c == 0) else (),
                           inc=TT if c == NCH - 1 else None)
                vh_mm[half] = S.val(TT)
            v_mm = S.val(TT)
            S.emit("scalar",
                   lambda s_, par=par: s_.activation(Kt[:, par * D:(par + 1) * D],
                                                     kp[par][:, :], AF.Sigmoid),
                   waits=[(TT, k_mm)], inc=SS)
            k_evac[tt] = S.val(SS)
            S.emit("vector",
                   lambda v, par=par: v.tensor_copy(Vt[:, par * D:par * D + 512],
                                                    vp2[par][:, 0:512]),
                   waits=[(TT, vh_mm[0])], inc=VV)
            S.emit("vector",
                   lambda v, par=par: v.tensor_copy(Vt[:, par * D + 512:(par + 1) * D],
                                                    vp2[par][:, 512:D]),
                   waits=[(TT, vh_mm[1])], inc=VV)
            v_evac[tt] = S.val(VV)
            kvw = [(SS, k_evac[tt]), (VV, max(v_evac[tt], ones_done, kvz))]
            if tt >= 1:
                kvw += [(VV, kv_add[tt - 1])]
            for h in range(H):
                def mmKV(t, h=h, par=par, tt=tt):
                    return t.matmul(kvp[par][0:64, h * DK:(h + 1) * DK],
                                    Kt[:, par * D + h * DK: par * D + (h + 1) * DK],
                                    Vt[:, par * D + h * DK: par * D + (h + 1) * DK],
                                    start=True, stop=True)
                S.emit("tensor", mmKV, waits=kvw if h == 0 else ())
            for half in range(2):
                def mmKS(t, half=half, par=par, tt=tt):
                    return t.matmul(kvp[par][64:65, half * 512:(half + 1) * 512],
                                    ones_b[:, :],
                                    Kt[:, par * D + half * 512: par * D + half * 512 + 512],
                                    start=True, stop=True)
                S.emit("tensor", mmKS, inc=TT if half == 1 else None)
            kv_mm = S.val(TT)
            S.emit("vector",
                   lambda v, par=par: v.tensor_tensor(kvacc[:, :], kvacc[:, :],
                                                      kvp[par][:, :], ALU.add),
                   waits=[(TT, kv_mm)], inc=VV)
            kv_add[tt] = S.val(VV)
        p1_t = S.val(TT)
        kv_evac = kv_add[NT - 1]
        ps1.close()

        gdma("g_pk", cc_in[:, :], kvacc[:, :], waits=[(VV, kv_evac)])
        pack_g = S.val("g_pk")

        def coll(g):
            return g.collective_compute(
                "AllReduce", ALU.add,
                replica_groups=[[0, 1], [2, 3], [4, 5], [6, 7]],
                ins=[cc_in[:, :]], outs=[cc_out[:, :]])
        S.emit("gpsimd", coll, waits=[("g_pk", pack_g)], inc=CC)

        S.emit("vector", lambda v: v.memset(BDf[:, :], 0.0),
               waits=[(TT, p1_t), (SS, k_evac[NT - 1])], inc=VV)
        S.emit("vector", lambda v: v.memset(KSf[:, :], 0.0), inc=VV)
        bdks_zero = S.val(VV)

        for h2 in range(2):
            ydma("y_sc",
                 mkap(BDf[:, :], [[1024, 64], [128, 8], [1, 64]], h2 * 64 * 1024 + h2 * 64),
                 mkap(cc_out[:, :], [[1024, 64], [128, 8], [1, 64]], h2 * 64),
                 waits=[(CC, 1), (VV, bdks_zero)] if h2 == 0 else ())
            S.emit("sync",
                   lambda sy, d=mkap(KSf[:, :], [[128, 64], [17, 8]],
                                     h2 * 64 * 128 + 8 * h2),
                   s_=mkap(cc_out[:, :], [[1, 64], [128, 8]], 64 * 1024 + h2 * 64):
                   sy.dma_start(out=d, in_=s_, allow_slow_non_contiguous=True),
                   inc="y_sc", n=16)
        sc_y = S.val("y_sc")
        S.emit("vector", lambda v: v.tensor_copy(KS_b[:, :], KSf[:, :]),
               waits=[("y_sc", sc_y)], inc=VV)
        ks_cast = S.val(VV)
        S.emit("vector", lambda v: v.tensor_copy(BD_b[:, :], BDf[:, :]), inc=VV)
        bd_cast = S.val(VV)

        if stage == "kv":
            ydma("y_dbg", dbg["kvtot"][0:64, :], cc_out[0:64, :], waits=[(VV, bd_cast)])
            ydma("y_dbg", dbg["kvtot"][64:65, :], cc_out[64:65, :])
            S.emit("sync", lambda sy: sy.nop(), waits=[("y_dbg", S.val("y_dbg"))])
            return finish(nc, S)

        psA = ExitStack()
        qp = [psA.enter_context(nc.psum_tensor(f"qp{i}", [128, 512], F32)) for i in range(2)]
        rbp = [psA.enter_context(nc.psum_tensor(f"rbp{i}", [128, 512], F32)) for i in range(2)]
        nump = [psA.enter_context(nc.psum_tensor(f"nump{i}", [128, 512], F32)) for i in range(2)]
        dnp = rbp[0][0:H, :]
        fcp = psA.enter_context(nc.psum_tensor("fcp", [128, D], F32))

        q_evac = {}
        fcb_evac = [0] * NT

        def emit_Q(n):
            npar = n % 2
            for m in range(NCH):
                idx = n * NCH + m
                w = []
                if idx == 0:
                    w = [(SS, k_evac[NT - 1]), (VV, kv_evac),
                         ("g_wq", ld_wq)]
                if idx >= 2:
                    w += [(SS, q_evac[idx - 2])]
                for c in range(NCH):
                    def mmQ(t, m=m, c=c, n=n):
                        return t.matmul(qp[m % 2][:, :],
                                        WQb[:, c * D + m * 128: c * D + m * 128 + 128],
                                        xTb[:, c * T + n * 512: c * T + n * 512 + 512],
                                        start=(c == 0), stop=(c == NCH - 1))
                    S.emit("tensor", mmQ, waits=w if c == 0 else (),
                           inc=TT if c == NCH - 1 else None)
                qmm = S.val(TT)
                S.emit("scalar",
                       lambda s_, m=m, npar=npar: s_.activation(
                           Qn[:, npar * 4096 + m * 512: npar * 4096 + (m + 1) * 512],
                           qp[m % 2][:, :], AF.Sigmoid),
                       waits=[(TT, qmm)], inc=SS)
                q_evac[idx] = S.val(SS)

        emit_Q(0)
        emit_Q(1)
        attn_prev = 0
        rb_prev = 0
        mrsP = [mrs, mrsB]
        recipd = {}

        def emit_den(n, extra=()):
            npar = n % 2
            dw = [(SS, q_evac[n * NCH + NCH - 1])] + list(extra)
            for c in range(NCH):
                def mmD(t, c=c, npar=npar):
                    return t.matmul(dnp[:, :],
                                    KS_b[:, c * H:(c + 1) * H],
                                    Qn[:, npar * 4096 + c * 512: npar * 4096 + (c + 1) * 512],
                                    start=(c == 0), stop=(c == NCH - 1))
                S.emit("tensor", mmD, waits=dw if c == 0 else (),
                       inc=TT if c == NCH - 1 else None)
            dmm = S.val(TT)
            S.emit("vector", lambda v: v.tensor_scalar(dtmp[:, :], dnp[:, :], 1.0, 1e-8,
                                                       ALU.mult, ALU.add),
                   waits=[(TT, dmm)], inc=VV)
            def vRC(v, n=n):
                with v.bass.allow_low_precision("bf16 matmul operand for rb bcast"):
                    return v.reciprocal(recip_b[:, n * 512:(n + 1) * 512], dtmp[:, :])
            S.emit("vector", vRC, inc=VV)
            recipd[n] = S.val(VV)

        emit_den(0, [(VV, ks_cast)])
        for n in range(NN):
            npar = n % 2
            recip_done = recipd[n]
            rb_evac = {}
            at_evac = {}
            for c in range(NCH):
                w = []
                if c <= 1:
                    w += [(VV, recip_done)]
                    if rb_prev:
                        w += [(SS, rb_prev)]
                    if n == 0 and c == 0:
                        w += [("g_eh", ld_ehead)]
                else:
                    w += [(SS, rb_evac[c - 2])]
                def mmRB(t, c=c, n=n):
                    return t.matmul(rbp[c % 2][:, :], ehead_b[:, c * 128:(c + 1) * 128],
                                    recip_b[:, n * 512:(n + 1) * 512],
                                    start=True, stop=True)
                S.emit("tensor", mmRB, waits=w, inc=TT)
                rbmm = S.val(TT)
                S.emit("scalar", lambda s_, c=c: s_.copy(mrsP[c % 2][:, :], rbp[c % 2][:, :]),
                       waits=[(TT, rbmm), (VV, at_evac.get(c - 2, attn_prev))], inc=SS)
                rb_evac[c] = S.val(SS)
                nw = []
                if c <= 1:
                    nw += [(VV, recip_done)]
                    if n == 0:
                        nw += [(VV, bd_cast)]
                else:
                    nw += [(VV, at_evac[c - 2])]
                def mmNUM(t, c=c, npar=npar):
                    return t.matmul(nump[c % 2][:, :], BD_b[:, c * 128:(c + 1) * 128],
                                    Qn[:, npar * 4096 + c * 512: npar * 4096 + (c + 1) * 512],
                                    start=True, stop=True)
                S.emit("tensor", mmNUM, waits=nw, inc=TT)
                nmm = S.val(TT)
                def vAT(v, c=c, npar=npar):
                    return v.scalar_tensor_tensor(
                        attn_n[:, npar * 4096 + c * 512: npar * 4096 + (c + 1) * 512],
                        nump[c % 2][:, :], 1.0, mrsP[c % 2][:, :], ALU.mult, ALU.mult)
                S.emit("vector", vAT, waits=[(TT, nmm), (SS, rb_evac[c])], inc=VV)
                at_evac[c] = S.val(VV)
            attn_done = at_evac[NCH - 1]
            attn_prev = attn_done
            rb_prev = rb_evac[NCH - 1]
            if n == NN - 1:
                num3_t = S.val(TT)
            if n + 1 < NN:
                emit_den(n + 1, [(SS, rb_evac[NCH - 1])])
            for i in range(4):
                tt = n * 4 + i
                w = [(VV, attn_done)]
                if n == 0 and i == 0:
                    w += [("g_wfc", ld_wfc)]
                if tt >= 1:
                    w += [(SS, fcb_evac[tt - 1])]
                for half in range(2):
                    for c in range(NCH):
                        def mmFC(t, c=c, half=half, i=i, npar=npar):
                            return t.matmul(
                                fcp[:, half * 512:(half + 1) * 512],
                                attn_n[:, npar * 4096 + c * 512 + i * 128:
                                       npar * 4096 + c * 512 + i * 128 + 128],
                                WFCb[:, c * D + half * 512: c * D + half * 512 + 512],
                                start=(c == 0), stop=(c == NCH - 1))
                        S.emit("tensor", mmFC,
                               waits=w if (half == 0 and c == 0) else (),
                               inc=TT if (half == 1 and c == NCH - 1) else None)
                fmm = S.val(TT)
                S.emit("scalar",
                       lambda s_, tt=tt: s_.copy(fcb[:, tt * D:(tt + 1) * D], fcp[:, :]),
                       waits=[(TT, fmm)], inc=SS)
                fcb_evac[tt] = S.val(SS)
            if n + 2 < NN:
                emit_Q(n + 2)
        pipea_t = S.val(TT)
        psA.close()

        gdma("g_w1", W1b[:, :].rearrange("p (e c h) -> p e c h", e=E, c=NCH),
             w1_h.rearrange("e (c p) h -> p e c h", p=128),
             waits=[(TT, p1_t)])
        w1_g = S.val("g_w1")

        ydma("y_gb", gam_r[:, :], gam_h[:, :], waits=[(TT, pipea_t)])
        ydma("y_gb", bet_r[:, :], bet_h[:, :])
        gb_y = S.val("y_gb")

        if stage == "fc":
            gdma("g_dbg", dbg["fcb"][:, :], fcb[:, :], waits=[(SS, fcb_evac[NT - 1])])
            S.emit("gpsimd", lambda g: g.nop(), waits=[("g_dbg", S.val("g_dbg"))])
            return finish(nc, S)


    gdma("g_w2", W2b[:, :].rearrange("p (e ch n) -> p e ch n", e=E, ch=2),
         w2_h.rearrange("e (ch p2) n -> p2 e ch n", p2=128),
         waits=[(TT, pipea_t)])
    w2_g = [S.val("g_w2")]

    psB = ExitStack()
    xppA = psB.enter_context(nc.psum_tensor("xppA", [128, D], F32))
    xppB = psB.enter_context(nc.psum_tensor("xppB", [128, D], F32))
    pb = psB.enter_context(nc.psum_tensor("pb", [128, 256], F32))
    bpp = psB.enter_context(nc.psum_tensor("bpp", [128, 2], F32))
    lgp = pb[:, 0:E]
    mkp = pb[0:E, 128:256]

    b1_bar = S.barrier(SS, VV) + ([(TT, num3_t)] if num3_t else [])
    if stage == "b1solo":
        b1_bar = b1_bar + [("g_fc", 16)]

    COLS = {"zs": 0, "nm1": 1, "t1": 2, "s1": 3, "r1": 4, "m1r1": 5,
            "z2s": 6, "nm2": 7, "a": 8, "sa": 9, "r2": 10, "ra": 11,
            "v2": 12, "v3": 13, "c3": 14, "s3": 15, "r3": 16, "r23": 17,
            "n2r2": 18, "n2r23": 19, "z3s": 20, "nm4": 21, "a4": 22,
            "s4": 23, "r4": 24, "n4r4": 25}

    def stc(name, tt):
        base = (tt % 2) * 32 + COLS[name]
        return st[:, base:base + 1]

    x_loaded = [0] * (NT + 2)
    x_free = [0] * NT
    mask_done = [0] * NT
    sq1_done = [0] * NT
    sq2_done = [0] * NT
    tp_done = [0] * NT
    xh_done = [0] * NT
    xh_val = [0] * NT
    hilo_d = [0] * NT
    mkd_d = [0] * NT
    for tt in range(2):
        xw = [(VV, kv_evac if stage == "b1solo" else max(kv_evac, bd_cast))]
        ydma(f"y_x{tt % 2}", x_tile[:, (tt % 2) * D:(tt % 2 + 1) * D],
             x_tm_h[tt * 128:(tt + 1) * 128, :], waits=xw)
        x_loaded[tt] = S.val(f"y_x{tt % 2}")

    def stretch_gen(tt):
        par = tt % 2
        zb_ = z_f32 if par == 0 else zB_f
        z2_ = z2_f32 if par == 0 else z2B_f
        xh_ = xh_f32 if par == 0 else xhB_f
        xpp = xppA if par == 0 else xppB
        w = [(f"y_x{par}", x_loaded[tt]), (SS, fcb_evac[tt])]
        if tt == 0:
            w += list(b1_bar)
        if tt >= 2:
            w += [(VV, x_free[tt - 2])]

        def vZ(v, tt=tt, par=par, zb_=zb_):
            return v.scalar_tensor_tensor(
                zb_[:, :], fcb[:, tt * D:(tt + 1) * D], 1.0,
                x_tile[:, par * D:(par + 1) * D],
                ALU.mult, ALU.add, accum_out=stc("zs", tt))
        S.emit("vector", vZ, waits=w, inc=VV)
        v1c = S.val(VV)
        S.emit("scalar", lambda s_, tt=tt: s_.mul(stc("nm1", tt), stc("zs", tt),
                                                  -1.0 / D),
               waits=[(VV, v1c)], inc=SS)
        def sSq(s_, tt=tt, zb_=zb_):
            return s_.activation(fcb[:, tt * D:(tt + 1) * D], zb_[:, :],
                                 AF.Square, bias=stc("nm1", tt),
                                 accum_out=stc("t1", tt))
        S.emit("scalar", sSq, inc=SS)
        sq1_done[tt] = S.val(SS)
        S.emit("scalar", lambda s_, tt=tt: s_.activation(
            stc("t1", tt), stc("t1", tt), AF.Copy, scale=1.0 / D, bias=EPS),
            inc=SS)
        S.emit("scalar", lambda s_, tt=tt: s_.activation(
            stc("s1", tt), stc("t1", tt), AF.Sqrt), inc=SS)
        s1d = S.val(SS)
        yield

        S.emit("vector", lambda v, tt=tt: v.reciprocal(stc("r1", tt), stc("s1", tt)),
               waits=[(SS, s1d)], inc=VV)
        S.emit("vector", lambda v, tt=tt: v.tensor_tensor(
            stc("m1r1", tt), stc("nm1", tt), stc("r1", tt), ALU.mult), inc=VV)
        S.emit("vector", lambda v, tt=tt, zb_=zb_: v.tensor_scalar(
            zb_[:, :], zb_[:, :], stc("r1", tt), stc("m1r1", tt),
            ALU.mult, ALU.add), inc=VV)
        S.emit("vector", lambda v, zb_=zb_: v.tensor_tensor(
            zb_[:, :], zb_[:, :], gam_r[:, :], ALU.mult),
            waits=[("y_gb", gb_y)], inc=VV)
        S.emit("vector", lambda v, zb_=zb_: v.tensor_tensor(
            zb_[:, :], zb_[:, :], bet_r[:, :], ALU.add), inc=VV)
        yield

        w2w = []
        if tt >= 2:
            w2w += [(SS, sq2_done[tt - 2]), (VV, xh_val[tt - 2])]
        def vZ2(v, tt=tt, par=par, zb_=zb_, z2_=z2_):
            return v.scalar_tensor_tensor(
                z2_[:, :], zb_[:, :], 1.0,
                x_tile[:, par * D:(par + 1) * D],
                ALU.mult, ALU.add, accum_out=stc("z2s", tt))
        S.emit("vector", vZ2, waits=w2w, inc=VV)
        x_free[tt] = S.val(VV)
        S.emit("scalar", lambda s_, tt=tt: s_.mul(stc("nm2", tt), stc("z2s", tt),
                                                  -1.0 / D),
               waits=[(VV, x_free[tt])], inc=SS)
        if tt + 2 < NT:
            ydma(f"y_x{par}", x_tile[:, par * D:(par + 1) * D],
                 x_tm_h[(tt + 2) * 128:(tt + 3) * 128, :],
                 waits=[(VV, x_free[tt])])
            x_loaded[tt + 2] = S.val(f"y_x{par}")
        def sSq2(s_, tt=tt, z2_=z2_):
            return s_.activation(fcb[:, tt * D:(tt + 1) * D], z2_[:, :],
                                 AF.Square, bias=stc("nm2", tt),
                                 accum_out=stc("a", tt))
        S.emit("scalar", sSq2, inc=SS)
        sq2_done[tt] = S.val(SS)
        yield

        S.emit("scalar", lambda s_, tt=tt: s_.mul(stc("v2", tt), stc("a", tt),
                                                  1.0 / D), inc=SS)
        S.emit("scalar", lambda s_, tt=tt: s_.activation(
            stc("a", tt), stc("v2", tt), AF.Copy, bias=EPS), inc=SS)
        S.emit("scalar", lambda s_, tt=tt: s_.activation(
            stc("sa", tt), stc("a", tt), AF.Sqrt), inc=SS)
        sad = S.val(SS)
        yield

        S.emit("vector", lambda v, tt=tt: v.reciprocal(stc("r2", tt), stc("sa", tt)),
               waits=[(SS, sad)], inc=VV)
        S.emit("vector", lambda v, tt=tt: v.reciprocal(stc("ra", tt), stc("a", tt)),
               inc=VV)
        rac = S.val(VV)
        yield

        S.emit("scalar", lambda s_, tt=tt: s_.activation(
            stc("v3", tt), stc("v2", tt), AF.Copy, scale=stc("ra", tt)),
            waits=[(VV, rac)], inc=SS)
        S.emit("scalar", lambda s_, tt=tt: s_.activation(
            stc("c3", tt), stc("v3", tt), AF.Copy, bias=EPS), inc=SS)
        S.emit("scalar", lambda s_, tt=tt: s_.activation(
            stc("s3", tt), stc("c3", tt), AF.Sqrt), inc=SS)
        s3d = S.val(SS)
        yield

        S.emit("vector", lambda v, tt=tt: v.reciprocal(stc("r3", tt), stc("s3", tt)),
               waits=[(SS, s3d)], inc=VV)
        S.emit("vector", lambda v, tt=tt: v.tensor_tensor(
            stc("r23", tt), stc("r2", tt), stc("r3", tt), ALU.mult), inc=VV)
        S.emit("vector", lambda v, tt=tt: v.tensor_tensor(
            stc("n2r2", tt), stc("nm2", tt), stc("r2", tt), ALU.mult), inc=VV)
        S.emit("vector", lambda v, tt=tt: v.tensor_tensor(
            stc("n2r23", tt), stc("nm2", tt), stc("r23", tt), ALU.mult), inc=VV)
        yield

        def vX1(v, tt=tt, z2_=z2_):
            return v.tensor_scalar(
                x1F[:, tt * D:(tt + 1) * D],
                z2_[:, :], stc("r2", tt), stc("n2r2", tt), ALU.mult, ALU.add)
        S.emit("vector", vX1, inc=VV)

        def vXH(v, tt=tt, z2_=z2_, xh_=xh_):
            return v.tensor_scalar(
                xh_[:, :], z2_[:, :], stc("r23", tt), stc("n2r23", tt),
                ALU.mult, ALU.add)
        S.emit("vector", vXH,
               waits=[(TT, tp_done[tt - 2])] if tt >= 2 else (), inc=VV)
        xh_val[tt] = S.val(VV)
        yield

        tw = [(VV, xh_val[tt]), ("y_s1", ld_sync1)]
        if tt >= 2:
            tw += [(SS, xh_done[tt - 2]), (VV, hilo_d[tt - 2])]
        for c in range(NCH):
            def mmT(t, c=c, xh_=xh_):
                return t.transpose(xpp[:, c * 128:(c + 1) * 128],
                                   xh_[:, c * 128:(c + 1) * 128], ident_f[:, :])
            S.emit("tensor", mmT, waits=tw if c == 0 else (),
                   inc=TT if c == NCH - 1 else None)
        tp_done[tt] = S.val(TT)
        yield

        def vXE(s_, tt=tt):
            q, i = tt // 4, tt % 4
            out_ap = mkap(xhatF[:, :], [[16384, 128], [512, 8], [1, 128]],
                          q * 4096 + i * 128)
            in_ap = xpp[:, :].rearrange("p (c n) -> p c n", c=NCH)
            return s_.copy(out_ap, in_ap)
        S.emit("scalar", vXE, waits=[(TT, tp_done[tt])], inc=SS)
        xh_done[tt] = S.val(SS)
        if "logits" in skips:
            mask_done[tt] = S.val(VV)
            hilo_d[tt] = S.val(VV)
            mkd_d[tt] = tp_done[tt]
            return
        def vLO(v, tt=tt):
            q, i = tt // 4, tt % 4
            hi_ap = mkap(xhatF[:, :], [[16384, 128], [512, 8], [1, 128]],
                         q * 4096 + i * 128)
            with v.bass.allow_low_precision("bf16x2 split"):
                return v.scalar_tensor_tensor(xlo_b[:, :], hi_ap, -1.0,
                                              xpp[:, :], ALU.mult, ALU.add)
        S.emit("vector", vLO, waits=[(SS, xh_done[tt])], inc=VV)
        hilo_d[tt] = S.val(VV)
        yield

        lw = [(VV, max(hilo_d[tt], wg_prep))]
        if tt >= 1:
            lw += [(VV, mask_done[tt - 1])]
        qq, ii = tt // 4, tt % 4
        lg_ops = []
        for c in range(NCH):
            hi_sl = (xhatF, qq * 4096 + c * 512 + ii * 128)
            lg_ops.append((hi_sl, wghi, c))
            lg_ops.append((hi_sl, wglo, c))
            lg_ops.append(((xlo_b, c * 128), wghi, c))
        for i2, ((xb_, xoff), wb_, c) in enumerate(lg_ops):
            def mmLG(t, xb_=xb_, xoff=xoff, wb_=wb_, c=c, i2=i2):
                return t.matmul(lgp, xb_[:, xoff:xoff + 128],
                                wb_[:, c * E:(c + 1) * E],
                                start=(i2 == 0), stop=(i2 == len(lg_ops) - 1))
            S.emit("tensor", mmLG, waits=lw if i2 == 0 else (),
                   inc=TT if i2 == len(lg_ops) - 1 else None)
        lgd = S.val(TT)
        S.emit("vector", lambda v: v.tensor_copy(lgt[:, :], lgp),
               waits=[(TT, lgd)], inc=VV)
        S.emit("vector", lambda v: v.reduce_max(
            mx8[:, 0:1], lgt[:, :], axis=mybir.AxisListType.X), inc=VV)
        mxd = S.val(VV)
        eqw = [(VV, mxd)]
        if tt >= 1:
            eqw += [(TT, mkd_d[tt - 1])]
        def vEQ(v):
            with v.bass.allow_low_precision("0/1 mask is exact in bf16"):
                return v.tensor_scalar(
                    mtm[:, :], lgt[:, :], mx8[:, 0:1], None, ALU.is_equal)
        S.emit("vector", vEQ, waits=eqw, inc=VV)
        mtmd = S.val(VV)
        mtw = [(VV, mtmd)]
        if tt >= 1:
            mtw += [(VV, mask_done[tt - 1])]
        S.emit("tensor", lambda t: t.transpose(mkp, mtm[:, :], ident_f[:, :]),
               waits=mtw, inc=TT)
        mkd_d[tt] = S.val(TT)
        S.emit("vector", lambda v, tt=tt: v.tensor_copy(
            masksfm[:, tt * 128:(tt + 1) * 128], mkp),
            waits=[(TT, mkd_d[tt])], inc=VV)
        mask_done[tt] = S.val(VV)

    w1s_done = 0
    bp_prev = [0, 0]
    def emit_w1prep(e):
        for ch in range(2):
            w = []
            if e == 0 and ch == 0:
                w = [("g_w1", w1_g), ("g_ct", ld_consts),
                     (SS, fcb_evac[NT - 1])]
            if bp_prev[ch]:
                w = list(w) + [(VV, bp_prev[ch])]
            for c in range(NCH):
                def mmBP(t, e=e, ch=ch, c=c):
                    col = (e * NCH + c) * HID + ch * 128
                    return t.matmul(bpp[:, ch:ch + 1],
                                    W1b[:, col:col + 128],
                                    lbT_b[:, c * E + e: c * E + e + 1],
                                    start=(c == 0), stop=(c == NCH - 1))
                S.emit("tensor", mmBP, waits=w if c == 0 else (),
                       inc=TT if c == NCH - 1 else None)
            bmm = S.val(TT)
            S.emit("vector",
                   lambda v, e=e, ch=ch: v.tensor_tensor(
                       bprime[:, e * 2 + ch: e * 2 + ch + 1],
                       bpp[:, ch:ch + 1], b1T_f[:, ch * E + e: ch * E + e + 1],
                       ALU.add),
                   waits=[(TT, bmm), ("y_s1", ld_sync1)], inc=VV)
            bp_prev[ch] = S.val(VV)
        bmm2 = S.val(TT)
        for c in range(NCH):
            def sSC(s_, e=e, c=c):
                col = (e * NCH + c) * HID
                return s_.activation(W1b[:, col:col + HID], W1b[:, col:col + HID],
                                     AF.Copy, scale=slsT_f[:, c * E + e: c * E + e + 1])
            S.emit("scalar", sSC, waits=[(TT, bmm2)] if c == 0 else (),
                   inc=SS if c == NCH - 1 else None)
        return S.val(SS)

    def drive(g):
        try:
            next(g)
        except StopIteration:
            pass

    prev_b = None
    for k in range(0, NT, 2):
        ga = stretch_gen(k)
        gb = stretch_gen(k + 1)
        drive(ga); drive(gb)
        if stage != "b1solo" and "w1prep" not in skips and k // 2 < E:
            w1s_done = emit_w1prep(k // 2)
        if prev_b is not None:
            drive(prev_b)
            for _ in range(8):
                drive(prev_b)
        for who in (ga, gb, ga, gb, ga, ga, ga, ga, ga, gb, gb, gb, gb, gb,
                    ga, gb, ga, ga, gb):
            drive(who)
        for _ in range(8):
            drive(ga)
        prev_b = gb
    if prev_b is not None:
        for _ in range(9):
            drive(prev_b)

    b1_v = S.val(VV)
    b1_s = S.val(SS)
    psB.close()

    if stage in ("x1", "b1solo"):
        gdma("g_dbg", dbg["x1"][:, :], x1F[:, :], waits=[(VV, b1_v), (SS, b1_s)])
        gdma("g_dbg", dbg["xhat"][:, :], xhatF[:, :])
        gdma("g_dbg", dbg["masks"][:, :], masksfm[:, :])
        S.emit("gpsimd", lambda g: g.nop(), waits=[("g_dbg", S.val("g_dbg"))])
        return finish(nc, S)

    psB2 = ExitStack()
    hp = [psB2.enter_context(nc.psum_tensor(f"hp{i}", [128, 512], F32))
          for i in range(4)]
    yp = [psB2.enter_context(nc.psum_tensor(f"yp{i}", [128, D], F32))
          for i in range(2)]

    b2_bar = [(VV, b1_v), (SS, b1_s)]

    z3_used = [0] * NT
    sq4_done = [0] * NT
    nm4_done = [0] * NT
    tail_done = {}
    out_dma = [0] * NT

    def emit_tail(tt):
        zb = z3a
        def sSq4(s_, tt=tt, zb=zb):
            return s_.activation(x1F[:, tt * D:(tt + 1) * D], zb[:, :],
                                 AF.Square, bias=stc("nm4", tt),
                                 accum_out=stc("a4", tt))
        S.emit("scalar", sSq4,
               waits=[(VV, z3_used[tt]), (SS, nm4_done[tt])], inc=SS)
        sq4c = S.val(SS)
        S.emit("scalar", lambda s_, tt=tt: s_.activation(
            stc("a4", tt), stc("a4", tt), AF.Copy, bias=EPS, scale=1.0 / D),
            waits=[(SS, sq4c)], inc=SS)
        S.emit("scalar", lambda s_, tt=tt: s_.activation(
            stc("s4", tt), stc("a4", tt), AF.Sqrt), inc=SS)
        sq4_done[tt] = S.val(SS)
        s4d = S.val(SS)
        S.emit("vector", lambda v, tt=tt: v.reciprocal(stc("r4", tt), stc("s4", tt)),
               waits=[(SS, s4d)], inc=VV)
        r4d = S.val(VV)
        S.emit("scalar", lambda s_, tt=tt: s_.activation(
            stc("n4r4", tt), stc("nm4", tt), AF.Copy, scale=stc("r4", tt)),
            waits=[(VV, r4d)], inc=SS)
        n4d = S.val(SS)
        ow = [(SS, n4d)]
        if tt >= 2:
            ow += [(f"y_o{tt % 2}", out_dma[tt - 2])]
        def vOUT(v, tt=tt, zb=zb, par=tt % 2):
            return v.tensor_scalar(
                outb[:, par * D:(par + 1) * D], zb[:, :],
                stc("r4", tt), stc("n4r4", tt), ALU.mult, ALU.add)
        S.emit("vector", vOUT, waits=ow + [(VV, r4d)], inc=VV)
        od = S.val(VV)
        tail_done[tt] = od
        ydma(f"y_o{tt % 2}", out_h[tt * 128:(tt + 1) * 128, :],
             outb[:, (tt % 2) * D:(tt % 2 + 1) * D], waits=[(VV, od)])
        out_dma[tt] = S.val(f"y_o{tt % 2}")

    for q in range(NN):
        gelu_d = {}
        msk_d = {}
        mrep_d = {}
        for e in range(E):
            epar = e % 2
            w = [(VV, mask_done[q * 4 + 3])]
            if q == 0 and e == 0:
                w += list(b2_bar)
            if e >= 2:
                w += [(SS, mrep_d[e - 2])]
            elif q >= 1:
                w += [(VV, z3_used[q * 4 - 2])]
            def mmMR(t, e=e, q=q, epar=epar):
                return t.matmul(yp[0][:, epar * 512:(epar + 1) * 512],
                                esel_b[:, e * 128:(e + 1) * 128],
                                masksfm[:, q * 512:(q + 1) * 512],
                                start=True, stop=True)
            S.emit("tensor", mmMR, waits=w, inc=TT)
            mrd = S.val(TT)
            S.emit("scalar", lambda s_, epar=epar: s_.copy(
                mrsP[epar][:, :], yp[0][:, epar * 512:(epar + 1) * 512]),
                   waits=[(TT, mrd), (VV, msk_d.get(e - 2, 0))], inc=SS)
            mrep_d[e] = S.val(SS)
            hw = [(SS, max(xh_done[q * 4 + 3], w1s_done))]
            if e >= 2:
                hw += [(SS, gelu_d[e - 2])]
            for ch in range(2):
                for c in range(NCH):
                    def mmH(t, e=e, ch=ch, c=c, q=q, epar=epar):
                        col = (e * NCH + c) * HID + ch * 128
                        return t.matmul(
                            hp[epar * 2 + ch][:, :],
                            W1b[:, col:col + 128],
                            xhatF[:, q * 4096 + c * 512: q * 4096 + (c + 1) * 512],
                            start=(c == 0), stop=(c == NCH - 1))
                    S.emit("tensor", mmH,
                           waits=hw if (ch == 0 and c == 0) else (),
                           inc=TT if (ch == 1 and c == NCH - 1) else None)
            hmm = S.val(TT)
            for ch in range(2):
                def sGE(s_, e=e, ch=ch, epar=epar):
                    return s_.activation(
                        hall[:, (e * 2 + ch) * 512:(e * 2 + ch + 1) * 512],
                        hp[epar * 2 + ch][:, :], AF.Gelu,
                        bias=bprime[:, e * 2 + ch:e * 2 + ch + 1])
                S.emit("scalar", sGE, waits=[(TT, hmm)] if ch == 0 else (),
                       inc=SS if ch == 1 else None)
            gelu_d[e] = S.val(SS)
            for ch in range(2):
                def vMM(v, e=e, ch=ch, epar=epar):
                    sl = hall[:, (e * 2 + ch) * 512:(e * 2 + ch + 1) * 512]
                    return v.tensor_tensor(sl, sl, mrsP[epar][:, :], ALU.mult)
                S.emit("vector", vMM,
                       waits=[(SS, max(gelu_d[e], mrep_d[e]))] if ch == 0 else (),
                       inc=VV if ch == 1 else None)
            msk_d[e] = S.val(VV)

        for i in range(4):
            tt = q * 4 + i
            ypt = yp[tt % 2]
            w = [(VV, msk_d[E - 1]), ("g_w2", max(w2_g)), ("g_ct", ld_consts),
                 (SS, mrep_d[E - 1])]
            if tt >= 2:
                w += [(VV, z3_used[tt - 2])]
            for nh in range(2):
                def mmB2(t, tt=tt, nh=nh, ypt=ypt):
                    return t.matmul(ypt[:, nh * 512:(nh + 1) * 512],
                                    masksfm[:, tt * 128:(tt + 1) * 128],
                                    b2b[:, nh * 512:(nh + 1) * 512],
                                    start=True, stop=False)
                S.emit("tensor", mmB2, waits=w if nh == 0 else ())
            for e in range(E):
                for ch in range(2):
                    for nh in range(2):
                        def mmY(t, e=e, ch=ch, nh=nh, i=i, ypt=ypt):
                            hcol = (e * 2 + ch) * 512 + i * 128
                            wcol = (e * 2 + ch) * D + nh * 512
                            return t.matmul(
                                ypt[:, nh * 512:(nh + 1) * 512],
                                hall[:, hcol:hcol + 128],
                                W2b[:, wcol:wcol + 512],
                                start=False,
                                stop=(e == E - 1 and ch == 1))
                        S.emit("tensor", mmY,
                               inc=TT if (e == E - 1 and ch == 1 and nh == 1) else None)
            ymm = S.val(TT)
            zb = z3a
            vz3w = [(TT, ymm)]
            if tt >= 1:
                vz3w += [(SS, sq4_done[tt - 1])]
            def vZ3(v, tt=tt, zb=zb, ypt=ypt):
                return v.scalar_tensor_tensor(
                    zb[:, :], ypt[:, :], 1.0,
                    x1F[:, tt * D:(tt + 1) * D],
                    ALU.mult, ALU.add, accum_out=stc("z3s", tt))
            S.emit("vector", vZ3, waits=vz3w, inc=VV)
            z3_used[tt] = S.val(VV)
            S.emit("scalar", lambda s_, tt=tt: s_.mul(
                stc("nm4", tt), stc("z3s", tt), -1.0 / D),
                waits=[(VV, z3_used[tt])], inc=SS)
            nm4_done[tt] = S.val(SS)
            emit_tail(tt)
    psB2.close()

    S.emit("sync", lambda sy: sy.nop(),
           waits=[("y_o0", S.val("y_o0")), ("y_o1", S.val("y_o1"))])
    return finish(nc, S)


def finish(nc, S):
    es = ExitStack()
    with es:
        names = set()
        for eng in S.ENGS:
            for waits, fn, inc, n in S.items[eng]:
                if inc is not None:
                    names.add(inc)
                for sname, _ in waits:
                    names.add(sname)
        sems = {}
        for name in sorted(names):
            sems[name] = es.enter_context(nc.semaphore(f"sem_{name}"))
        block = es.enter_context(nc.Block())

        @block.gpsimd
        def _(g):
            S.replay("gpsimd", g, sems)

        @block.tensor
        def _(t):
            S.replay("tensor", t, sems)

        @block.vector
        def _(v):
            S.replay("vector", v, sems)

        @block.scalar
        def _(s_):
            S.replay("scalar", s_, sems)

        @block.sync
        def _(sy):
            S.replay("sync", sy, sems)
    return nc


def make_inputs(full):
    x = np.asarray(full["L0_attn_inputs"], np.float32)
    C = np.ascontiguousarray
    shared = dict(
        wq=C(np.asarray(full["L0_attn_W_Q"], np.float32)),
        wk=C(np.asarray(full["L0_attn_W_K"], np.float32)),
        wv=C(np.asarray(full["L0_attn_W_V"], np.float32)),
        wfc=C(np.asarray(full["L0_attn_W_fc"], np.float32)),
        gam=C(np.broadcast_to(np.asarray(full["L0_attn_gamma"], np.float32), (128, D))),
        bet=C(np.broadcast_to(np.asarray(full["L0_attn_beta"], np.float32), (128, D))),
        wgT=C(np.asarray(full["L0_ffn_Wg"], np.float32).T),
        slsT=C(np.asarray(full["L0_ffn_ln_scale"], np.float32).T),
        lbT=C(np.asarray(full["L0_ffn_ln_bias"], np.float32).T),
        w1=C(np.asarray(full["L0_ffn_W1"], np.float32)),
        b1T=C(np.asarray(full["L0_ffn_b1"], np.float32).T),
        w2=C(np.asarray(full["L0_ffn_W2"], np.float32)),
        b2=C(np.asarray(full["L0_ffn_b2"], np.float32)),
        ident=np.eye(128, dtype=np.float32),
        iota8=C(np.broadcast_to(np.arange(E, dtype=np.float32), (128, E))),
    )
    esel = np.zeros((E, E * 128), np.float32)
    for e in range(E):
        esel[e, e * 128:(e + 1) * 128] = 1.0
    ehead = np.zeros((H, NCH * 128), np.float32)
    for c in range(NCH):
        ehead[c, c * 128:c * 128 + 64] = 1.0
        ehead[8 + c, c * 128 + 64:(c + 1) * 128] = 1.0
    shared["esel"] = esel
    shared["ehead"] = ehead

    in_maps = []
    for core in range(8):
        b, hh = core // 2, core % 2
        sl = x[b, hh * T:(hh + 1) * T, :]
        in_maps.append(dict(shared, x_tm=C(sl), x_fm=C(sl.T)))
    return in_maps


def kernel(**inputs):
    in_maps = make_inputs(inputs)
    nc = build("full")
    res = run_bass_kernel_spmd(nc, in_maps, core_ids=list(range(8)))
    out = np.empty((4, 4096, D), np.float32)
    for core in range(8):
        b, hh = core // 2, core % 2
        out[b, hh * T:(hh + 1) * T, :] = res.results[core]["out"]
    return out
